# version 20 (final)
# baseline (speedup 1.0000x reference)
import numpy as np
import concourse.bass as bass
import concourse.mybir as mybir
from concourse.bass_utils import run_bass_kernel_spmd

F32 = mybir.dt.float32
BF16 = mybir.dt.bfloat16
AX = mybir.AxisListType.X
ALU = mybir.AluOpType
AF = mybir.ActivationFunctionType

P = 128
D = 2048
S = 2048
NCH = 16
BLK = 512
NBLK = 4
DKV = 512
NQG = 4
NH = 8
DH = 64
IFF = 1408
ICH = 11
NEXP = 10
EPS = 1e-5
SCALE = 0.125

GROUPS = [(0, 3), (3, 6), (6, 9), (9, 12), (12, 15), (15, 16)]


def _sz(dtype):
    return 2 if dtype == BF16 else 4


class Arena:
    def __init__(self, nc, base, limit):
        self.nc, self.off, self.limit = nc, base, limit

    def sb(self, name, shape, dtype):
        nbytes = int(np.prod(shape[1:])) * _sz(dtype)
        t = self.nc.alloc_sbuf_tensor_at(name, list(shape), dtype, offset=self.off)
        self.off += (nbytes + 63) & ~63
        assert self.off <= self.limit, f"SBUF overflow at {name}: {self.off}"
        return t


class Ctx:
    pass


def bump(c, nm, k=1):
    setattr(c, nm + "_n", getattr(c, nm + "_n") + k)
    return getattr(c, nm + "_n")


class EngProxy:

    def __init__(self, c, eng, sem_name):
        self._c = c
        self.raw = eng
        self._sem = getattr(c, sem_name)
        self._nm = sem_name

    def __getattr__(self, name):
        fn = getattr(self.raw, name)

        def call(*a, **k):
            c = self._c
            self.raw.wait_ge(self._sem, getattr(c, self._nm + "_n"))
            ins = fn(*a, **k)
            ins.then_inc(self._sem, 1)
            bump(c, self._nm)
            return ins

        return call


def build_program(taps=(), phases=("P", "A", "B", "C")):
    nc = bass.Bass("TRN2", target_bir_lowering=False, debug=False)
    c = Ctx()
    c.nc = nc
    c.taps = set(taps)

    def din(name, shape):
        return nc.dram_tensor(name, list(shape), F32, kind="ExternalInput").ap()

    c.xT = din("xT", [D, S])
    c.wqT = din("wqT", [D, D])
    c.wkT = din("wkT", [D, DKV])
    c.wvT = din("wvT", [D, DKV])
    c.mem = din("mem", [DKV, DKV])
    c.memnorm = din("memnorm", [DKV])
    c.mweight = din("mweight", [1])
    c.routerT = din("routerT", [D, 8])
    c.gate = din("gate", [8, D, IFF])
    c.up = din("up", [8, D, IFF])
    c.down = din("down", [8, IFF, D])
    c.sgate = din("sgate", [D, 2 * IFF])
    c.sup = din("sup", [D, 2 * IFF])
    c.sdown = din("sdown", [2 * IFF, D])
    c.sgb = din("sgb", [2 * IFF])
    c.supb = din("supb", [2 * IFF])
    c.sdb = din("sdb", [D])
    c.outT = nc.dram_tensor("outT", [D, S], F32, kind="ExternalOutput").ap()

    c.xln1T_scr = nc.dram_tensor("xln1T_scr", [P, NCH, S], BF16, kind="Internal").ap()
    c.wq_scr = nc.dram_tensor("wq_scr", [P, NCH, D], BF16, kind="Internal").ap()
    c.x2_scr = nc.dram_tensor("x2_scr", [P, NCH, S], F32, kind="Internal").ap()
    c.gu_scr = nc.dram_tensor("gu_scr", [P, 320, IFF], BF16, kind="Internal").ap()
    c.dn_scr = nc.dram_tensor("dn_scr", [P, 110, D], BF16, kind="Internal").ap()

    c.psA = nc.alloc_psum_tensor("psA", [P, 1536], F32)
    c.psB = nc.alloc_psum_tensor("psB", [P, 1536], F32)
    c.psC = nc.alloc_psum_tensor("psC", [P, 512], F32)
    c.psD = nc.alloc_psum_tensor("psD", [P, 512], F32)

    BASE0 = (nc._sbuf_addr_for_side("left") + 63) & ~63
    LIM = nc._sbuf_addr_for_side("right") - 64
    per = Arena(nc, BASE0, LIM)
    c.ones_row = per.sb("ones_row", [1, P], F32)
    c.ones_col_bf = per.sb("ones_col_bf", [P, 1], BF16)
    c.one_one = per.sb("one_one", [1, 1], F32)
    c.w_ap = per.sb("w_ap", [1, 1], F32)
    c.lnw_ap = per.sb("lnw_ap", [1, 1], F32)
    c.w1m_col = per.sb("w1m_col", [1, DH], F32)
    c.w1m_cT = per.sb("w1m_cT", [P, 1], F32)
    c.w_cT = per.sb("w_cT", [P, 1], F32)
    c.ident = per.sb("ident", [P, P], F32)
    c.iota = per.sb("iota", [P, P], F32)
    c.mem_sb = per.sb("mem_sb", [P, 4 * 512], BF16)
    c.mn_sb = per.sb("mn_sb", [P, 4], BF16)
    c.routerT_sb = per.sb("routerT_sb", [P, NCH * 8], BF16)
    c.gb_col = per.sb("gb_col", [P, 22], F32)
    c.ub_col = per.sb("ub_col", [P, 22], F32)
    c.db_col = per.sb("db_col", [P, NCH], F32)
    c.mwrow = per.sb("mwrow", [1, 1], F32)
    c.rows = per.sb("rows", [1, 8 * 512 + 16], F32)
    PBASE = (per.off + 63) & ~63

    ab = Arena(nc, PBASE, LIM)
    c.xblk = ab.sb("xblk", [P, NCH * 512], F32)
    c.xln1T = ab.sb("xln1T", [P, NCH * 512], BF16)
    c.kT = ab.sb("kT", [P, 4 * S], BF16)
    c.v_aug = ab.sb("v_aug", [P, NCH * 640 + 64], BF16)
    c.tbf0 = ab.sb("tbf0", [P, 512], BF16)
    c.tbf1 = ab.sb("tbf1", [P, 512], BF16)
    c.tsq0 = ab.sb("tsq0", [P, 512], BF16)
    c.tsq1 = ab.sb("tsq1", [P, 512], BF16)
    c.tmpA = ab.sb("tmpA", [P, 512], F32)
    AB_SHARED_END = ab.off
    a_ar = Arena(nc, AB_SHARED_END, LIM)
    c.wkT_sb = a_ar.sb("wkT_sb", [P, NCH * 512], BF16)
    c.wvT_sb = a_ar.sb("wvT_sb", [P, NCH * 512], BF16)
    b_ar = Arena(nc, AB_SHARED_END, LIM)
    c.qT = b_ar.sb("qT", [P, NCH * 512], BF16)
    pe0_off = b_ar.off
    c.probs_e0 = b_ar.sb("probs_e0", [P, NCH * 512], BF16)
    pe1_off = b_ar.off
    c.probs_e1 = b_ar.sb("probs_e1", [P, NCH * 512], BF16)
    po0_off = b_ar.off
    c.probs_o0 = b_ar.sb("probs_o0", [P, NCH * 512], BF16)
    c.probs_o1 = b_ar.sb("probs_o1", [P, NCH * 512], BF16)
    c.wq_kc0 = b_ar.sb("wq_kc0", [P, D], BF16)
    c.wq_kc1 = b_ar.sb("wq_kc1", [P, D], BF16)
    c.rzB_sb = b_ar.sb("rzB_sb", [DH, 512], F32)
    c.tmp2a = b_ar.sb("tmp2a", [DH, 512], F32)
    c.tmp2b = b_ar.sb("tmp2b", [DH, 512], F32)
    al1 = Arena(nc, pe0_off, LIM)
    c.sigmaT = al1.sb("sigmaT", [P, NCH * 512], BF16)
    al2 = Arena(nc, pe1_off, LIM)
    c.et1 = al2.sb("et1", [P, 1024], F32)
    c.et2 = al2.sb("et2", [P, 1024], F32)
    al3 = Arena(nc, po0_off, LIM)
    c.amB = al3.sb("amB", [P, 512], F32)
    AB_END = max(a_ar.off, b_ar.off)

    c_ar = Arena(nc, PBASE, LIM)
    c.acc = c_ar.sb("acc", [P, NCH * 512], F32)
    c.xln2T = c_ar.sb("xln2T", [P, NCH * 512], BF16)
    c.slot0 = c_ar.sb("slot0", [P, NCH * IFF], BF16)
    c.slot1 = c_ar.sb("slot1", [P, NCH * IFF], BF16)
    c.silu_g = c_ar.sb("silu_g", [P, ICH * 512], BF16)
    c.h_sb = c_ar.sb("h_sb", [P, ICH * 512], BF16)
    c.cbB = c_ar.sb("cbB", [P, 512], BF16)
    c.combine = c_ar.sb("combine", [P, 32], F32)
    c.cb_row = c_ar.sb("cb_row", [1, 512], F32)
    c.ctbf0 = c_ar.sb("ctbf0", [P, 512], BF16)
    c.ctbf1 = c_ar.sb("ctbf1", [P, 512], BF16)
    c.ctsq0 = c_ar.sb("ctsq0", [P, 512], BF16)
    c.ctsq1 = c_ar.sb("ctsq1", [P, 512], BF16)
    c.ctmp = c_ar.sb("ctmp", [P, 512], F32)
    c.sx0 = c_ar.sb("sx0", [P, 512], F32)
    c.sx1 = c_ar.sb("sx1", [P, 512], F32)
    c.xs2 = c_ar.sb("xs2", [P, 512], F32)
    c.xs3 = c_ar.sb("xs3", [P, 512], F32)
    c.xs_bufs = [c.sx0, c.sx1, c.xs2, c.xs3]
    c.sxb0 = c_ar.sb("sxb0", [P, 512], BF16)
    c.sxb1 = c_ar.sb("sxb1", [P, 512], BF16)
    c.sxq0 = c_ar.sb("sxq0", [P, 512], BF16)
    c.sxq1 = c_ar.sb("sxq1", [P, 512], BF16)
    c.aB_sb = c_ar.sb("aB_sb", [P, 512], F32)
    c.bB_sb = c_ar.sb("bB_sb", [P, 512], F32)

    c.exp32 = c_ar.sb("exp32", [P, 32], F32)
    c.r8a = c_ar.sb("r8a", [P, 8], F32)
    c.r8b = c_ar.sb("r8b", [P, 8], F32)
    c.r1a = c_ar.sb("r1a", [P, 4], F32)
    C_END = c_ar.off
    assert max(AB_END, C_END) <= LIM, (AB_END, C_END)

    pr = Arena(nc, AB_SHARED_END, LIM)
    c.row_big = pr.sb("row_big", [1, 2 * IFF], F32)
    c.stage0 = pr.sb("stage0", [P, D], BF16)
    c.stage1 = pr.sb("stage1", [P, D], BF16)

    for nm in ("s_x", "s_wm", "s_wg", "s_wu", "s_wd", "s_wq0", "s_wq1",
               "s_sc", "s_acc", "s_o", "s_cv", "s_se0", "s_se1",
               "s_xs0", "s_xs1",
               "s_pe", "s_dv", "s_ac", "s_gp", "s_bar"):
        setattr(c, nm, nc.alloc_semaphore(nm))
        setattr(c, nm + "_n", 0)
    c.s_ob = [nc.alloc_semaphore("s_ob%d" % b) for b in range(4)]
    c.s_ob_n = [0, 0, 0, 0]
    c.u_pe_end = 0
    c.last_cb_dv = 0
    c.bcast_ac_done = 0
    c.tap_list = []
    c.conv_chunks = _conv_chunks(c)
    c.conv_cursor = 0

    with nc.Block() as bigblk:
        if "P" in phases:
            emit_prologue(c, bigblk)
            barrier(c, bigblk)
        if "A" in phases:
            for blk in range(NBLK):
                emit_phase_a(c, bigblk, blk)
            barrier(c, bigblk)
        if "B" in phases:
            for blk in range(NBLK):
                emit_phase_b(c, bigblk, blk)
            barrier(c, bigblk)
        if "C" in phases:
            for st in range(NBLK):
                emit_phase_c(c, bigblk, st)
        emit_epilogue(c, bigblk)
    return nc, c


def barrier(c, bigblk):
    tgt = c.s_bar_n + 5
    for name in ("tensor", "vector", "scalar", "gpsimd", "sync"):
        def body(eng, _t=tgt):
            eng.nop().then_inc(c.s_bar, 1)
            eng.wait_ge(c.s_bar, _t)
        getattr(bigblk, name)(body)
    c.s_bar_n = tgt


def _conv_chunks(c):
    out = []
    for e in range(NEXP):
        shared = e >= 8
        half = e - 8
        if not shared:
            g = c.gate[e].rearrange("(cc p) i -> p cc i", p=P)
            u = c.up[e].rearrange("(cc p) i -> p cc i", p=P)
            d = c.down[e].rearrange("(cc p) o -> p cc o", p=P)
        else:
            g = c.sgate.rearrange("(cc p) i -> p cc i", p=P)[:, :, half * IFF:(half + 1) * IFF]
            u = c.sup.rearrange("(cc p) i -> p cc i", p=P)[:, :, half * IFF:(half + 1) * IFF]
            d = c.sdown[half * IFF:(half + 1) * IFF, :].rearrange("(cc p) o -> p cc o", p=P)
        for kc in range(NCH):
            out.append((g[:, kc, :], c.gu_scr[:, 32 * e + kc, :], IFF))
        for kc in range(NCH):
            out.append((u[:, kc, :], c.gu_scr[:, 32 * e + 16 + kc, :], IFF))
        for ic in range(ICH):
            out.append((d[:, ic, :], c.dn_scr[:, 11 * e + ic, :], D))
    return out


def emit_convert_batch(c, bigblk, lo, hi, pe_gate=None):
    chunks = c.conv_chunks[lo:hi]
    if not chunks:
        return

    def gb(g):
        for (srcap, dstap, width) in chunks:
            g.dma_start(dstap, srcap).then_inc(c.s_cv, 16)
            bump(c, "s_cv", 16)
    bigblk.gpsimd(gb)



def emit_prologue(c, bigblk):
    nc = c.nc
    blk = bigblk
    if True:
        @blk.sync
        def _(sy):
            sy.dma_start(c.mwrow[0:1, :], c.mweight[None, :]).then_inc(c.s_x, 16)
            sy.dma_start(c.rows[0:1, 0:DKV], c.memnorm[None, :]).then_inc(c.s_x, 16)
            sy.dma_start(c.row_big[0:1, :], c.sgb[None, :]).then_inc(c.s_x, 16)
        bump(c, "s_x", 48)

        @blk.gpsimd
        def _(g):
            gp = EngProxy(c, g, "s_gp")
            gp.iota(c.iota[:, :], pattern=[[1, P]], base=0, channel_multiplier=-1,
                    allow_small_or_imprecise_dtypes=True)
            g.dma_start(c.mem_sb[:, :].rearrange("p (c o) -> p c o", c=4),
                        c.mem.rearrange("(c p) o -> p c o", p=P)).then_inc(c.s_wm, 16)
            g.dma_start(c.routerT_sb[:, :].rearrange("p (c e) -> p c e", c=NCH),
                        c.routerT.rearrange("(c p) e -> p c e", p=P)).then_inc(c.s_wm, 16)
            c_w = 32
            for kc in range(NCH):
                stage = (c.stage0, c.stage1)[kc % 2]
                if kc >= 2:
                    g.wait_ge(c.s_sc, 16 * kc)
                g.dma_start(stage[:, :], c.wqT.rearrange("(c p) o -> p c o", p=P)[:, kc, :]
                            ).then_inc(c.s_wm, 16)
                c_w += 16
                g.wait_ge(c.s_wm, c_w)
                g.dma_start(c.wq_scr[:, kc, :], stage[:, :]).then_inc(c.s_sc, 16)
        bump(c, "s_wm", 32 + 16 * NCH)
        bump(c, "s_sc", 16 * NCH)
        gp_iota = c.s_gp_n

        @blk.scalar
        def _(s):
            sp = EngProxy(c, s, "s_ac")
            s.wait_ge(c.s_x, 48)
            sp.activation(c.w_ap[0:1, :], c.mwrow[0:1, :], AF.Sigmoid)
            sp.activation(c.lnw_ap[0:1, :], c.w_ap[0:1, :], AF.Ln)

        @blk.vector
        def _(v):
            vp = EngProxy(c, v, "s_dv")
            vp.memset(c.ones_row[:, :], 1.0)
            vp.memset(c.ones_col_bf[:, :], 1.0)
            vp.memset(c.one_one[:, :], 1.0)
            vp.memset(c.v_aug[:, :], 1.0)
            v.wait_ge(c.s_gp, gp_iota)
            vp.tensor_single_scalar(c.ident[:, :], c.iota[:, :], 0.0, ALU.is_equal)
            v.wait_ge(c.s_ac, 1)
            vp.tensor_scalar(c.w1m_col[0:1, :], c.ones_row[0:1, 0:DH], c.w_ap[0:1, 0:1],
                             None, ALU.subtract)
            v.wait_ge(c.s_pe, 1)
            vp.tensor_copy(c.mn_sb[:, 0:4], c.psC[:, 0:4])
            v.wait_ge(c.s_pe, 2)
            vp.tensor_copy(c.gb_col[:, :], c.psC[:, 0:22])
            v.wait_ge(c.s_pe, 3)
            vp.tensor_copy(c.ub_col[:, :], c.psC[:, 0:22])
            v.wait_ge(c.s_pe, 4)
            vp.tensor_copy(c.db_col[:, :], c.psC[:, 0:16])
            vp.tensor_scalar(c.rows[0:1, 4096:4097], c.one_one[0:1, 0:1],
                             c.w_ap[0:1, 0:1], None, ALU.subtract)
        @blk.tensor
        def _(t):
            t.wait_ge(c.s_x, 48)
            t.wait_ge(c.s_dv, 4)
            for j in range(4):
                ins = t.matmul(c.psC[:, j:j + 1], c.rows[0:1, j * P:(j + 1) * P],
                               c.one_one[0:1, 0:1], start=True, stop=True)
            ins.then_inc(c.s_pe, 1)
            t.wait_ge(c.s_dv, 7)
            for j in range(22):
                ins = t.matmul(c.psC[:, j:j + 1], c.row_big[0:1, j * P:(j + 1) * P],
                               c.one_one[0:1, 0:1], start=True, stop=True)
            ins.then_inc(c.s_pe, 1)
        bump(c, "s_pe", 2)

    blk2 = bigblk
    if True:
        @blk2.sync
        def _(sy):
            sy.wait_ge(c.s_pe, 2)
            sy.dma_start(c.row_big[0:1, :], c.supb[None, :]).then_inc(c.s_x, 16)
            sy.wait_ge(c.s_pe, 3)
            sy.dma_start(c.row_big[0:1, 0:D], c.sdb[None, :]).then_inc(c.s_x, 16)
        bump(c, "s_x", 32)

        @blk2.tensor
        def _(t):
            t.wait_ge(c.s_x, 64)
            t.wait_ge(c.s_dv, 8)
            for j in range(22):
                ins = t.matmul(c.psC[:, j:j + 1], c.row_big[0:1, j * P:(j + 1) * P],
                               c.one_one[0:1, 0:1], start=True, stop=True)
            ins.then_inc(c.s_pe, 1)
            t.wait_ge(c.s_x, 80)
            t.wait_ge(c.s_dv, 9)
            for j in range(16):
                ins = t.matmul(c.psC[:, j:j + 1], c.row_big[0:1, j * P:(j + 1) * P],
                               c.one_one[0:1, 0:1], start=True, stop=True)
            ins.then_inc(c.s_pe, 1)
        bump(c, "s_pe", 2)

        dv_w1 = c.s_dv_n
        pe_w0 = c.s_pe_n

        @blk2.tensor
        def _(t):
            t.wait_ge(c.s_dv, dv_w1)
            t.matmul(c.psC[:, 0:1], c.ones_row[0:1, :], c.rows[0:1, 4096:4097],
                     start=True, stop=True).then_inc(c.s_pe, 1)
            t.matmul(c.psC[:, 1:2], c.ones_row[0:1, :], c.w_ap[0:1, 0:1],
                     start=True, stop=True).then_inc(c.s_pe, 1)
        bump(c, "s_pe", 2)

        @blk2.vector
        def _(v):
            vp = EngProxy(c, v, "s_dv")
            v.wait_ge(c.s_pe, pe_w0 + 2)
            vp.tensor_copy(c.w1m_cT[:, :], c.psC[:, 0:1])
            vp.tensor_copy(c.w_cT[:, :], c.psC[:, 1:2])



def make_ln_bodies(c, src, dst, bufs, pre_wait_v=None, pre_wait_s=None,
                   dst_war_wait=None, chunk_wait_v=None, chunk_wait_s=None):
    tbf = (bufs[0], bufs[1])
    tsq = (bufs[2], bufs[3])
    tmp = bufs[4]
    rows = c.rows
    mrow = rows[0:1, 0:512]
    vrow = rows[0:1, 512:1024]
    msq = rows[0:1, 1024:1536]
    arow = rows[0:1, 1536:2048]
    brow = rows[0:1, 2048:2560]

    dv0 = c.s_dv_n
    ac0 = c.s_ac_n
    pe0 = c.s_pe_n

    def vbody(v):
        vp = EngProxy(c, v, "s_dv")
        if pre_wait_v:
            pre_wait_v(v)
        for ch in range(NCH):
            if chunk_wait_v:
                chunk_wait_v(v, ch)
            if ch >= 2:
                v.wait_ge(c.s_pe, pe0 + 2 * (ch - 1))
            vp.tensor_copy(tbf[ch % 2][:, :], src[:, 512 * ch:512 * (ch + 1)])
        v.wait_ge(c.s_pe, pe0 + 32)
        vp.tensor_scalar(mrow, c.psC[0:1, :], 1.0 / D, None, ALU.mult)
        vp.tensor_tensor(msq, mrow, mrow, ALU.mult)
        vp.tensor_scalar(vrow, c.psD[0:1, :], 1.0 / D, EPS, ALU.mult, ALU.add)
        vp.tensor_tensor(vrow, vrow, msq, ALU.subtract)
        v.wait_ge(c.s_ac, ac0 + 18)
        vp.scalar_tensor_tensor(brow, mrow, -1.0, arow, ALU.mult, ALU.mult)
        v.wait_ge(c.s_pe, pe0 + 34)
        if dst_war_wait:
            dst_war_wait(v)
        for ch in range(NCH):
            vp.tensor_tensor(tmp[:, :], src[:, 512 * ch:512 * (ch + 1)],
                             c.psC[:, :], ALU.mult)
            vp.tensor_tensor(dst[:, 512 * ch:512 * (ch + 1)], tmp[:, :],
                             c.psD[:, :], ALU.add)

    def sbody(s):
        sp = EngProxy(c, s, "s_ac")
        if pre_wait_s:
            pre_wait_s(s)
        for ch in range(NCH):
            if chunk_wait_s:
                chunk_wait_s(s, ch)
            if ch >= 2:
                s.wait_ge(c.s_pe, pe0 + 2 * ch)
            sp.activation(tsq[ch % 2][:, :], src[:, 512 * ch:512 * (ch + 1)], AF.Square)
        s.wait_ge(c.s_dv, dv0 + 20)
        sp.activation(arow, vrow, AF.Ln)
        sp.activation(arow, arow, AF.Exp, scale=-0.5)

    def tbody(t):
        for ch in range(NCH):
            t.wait_ge(c.s_dv, dv0 + ch + 1)
            t.matmul(c.psC[0:1, :], c.ones_col_bf[:, 0:1], tbf[ch % 2][:, :],
                     start=(ch == 0), stop=(ch == NCH - 1)).then_inc(c.s_pe, 1)
            bump(c, "s_pe")
            t.wait_ge(c.s_ac, ac0 + ch + 1)
            t.matmul(c.psD[0:1, :], c.ones_col_bf[:, 0:1], tsq[ch % 2][:, :],
                     start=(ch == 0), stop=(ch == NCH - 1)).then_inc(c.s_pe, 1)
            bump(c, "s_pe")
        t.wait_ge(c.s_dv, dv0 + 21)
        t.matmul(c.psC[:, :], c.ones_row[0:1, :], arow, start=True, stop=True
                 ).then_inc(c.s_pe, 1)
        bump(c, "s_pe")
        t.matmul(c.psD[:, :], c.ones_row[0:1, :], brow, start=True, stop=True
                 ).then_inc(c.s_pe, 1)
        bump(c, "s_pe")

    return vbody, sbody, tbody


def emit_c_stats_early(c, bigblk, st):
    scol = st * BLK
    se0_0 = c.s_se0_n
    se1_0 = c.s_se1_n
    dv0 = c.s_dv_n
    ac0 = c.s_ac_n
    pe0 = c.s_pe_n

    def sy(s):
        for ch in range(NCH):
            par = ch % 2
            if ch >= 2:
                s.wait_ge(c.s_dv, dv0 + ch - 1)
                s.wait_ge(c.s_ac, ac0 + ch - 1)
            s.dma_start((c.sx0, c.sx1)[par][:, :],
                        c.x2_scr[:, ch, scol:scol + BLK]
                        ).then_inc((c.s_se0, c.s_se1)[par], 16)
            bump(c, "s_se0" if par == 0 else "s_se1", 16)
    bigblk.sync(sy)

    def vb(v):
        vp = EngProxy(c, v, "s_dv")
        for ch in range(NCH):
            par = ch % 2
            v.wait_ge((c.s_se0, c.s_se1)[par],
                      (se0_0, se1_0)[par] + 16 * (ch // 2 + 1))
            if ch >= 2:
                v.wait_ge(c.s_pe, pe0 + 2 * (ch - 2) + 1)
            vp.tensor_copy((c.sxb0, c.sxb1)[par][:, :], (c.sx0, c.sx1)[par][:, :])
    bigblk.vector(vb)

    def sb(s):
        sp = EngProxy(c, s, "s_ac")
        for ch in range(NCH):
            par = ch % 2
            s.wait_ge((c.s_se0, c.s_se1)[par],
                      (se0_0, se1_0)[par] + 16 * (ch // 2 + 1))
            if ch >= 2:
                s.wait_ge(c.s_pe, pe0 + 2 * (ch - 2) + 2)
            sp.activation((c.sxq0, c.sxq1)[par][:, :], (c.sx0, c.sx1)[par][:, :],
                          AF.Square)
    bigblk.scalar(sb)

    def tb(t):
        for ch in range(NCH):
            par = ch % 2
            t.wait_ge(c.s_dv, dv0 + ch + 1)
            t.matmul(c.psC[0:1, :], c.ones_col_bf[:, 0:1], (c.sxb0, c.sxb1)[par][:, :],
                     start=(ch == 0), stop=(ch == NCH - 1),
                     skip_group_check=True).then_inc(c.s_pe, 1)
            bump(c, "s_pe")
            t.wait_ge(c.s_ac, ac0 + ch + 1)
            t.matmul(c.psC[32:33, :], c.ones_col_bf[:, 0:1], (c.sxq0, c.sxq1)[par][:, :],
                     start=(ch == 0), stop=(ch == NCH - 1),
                     skip_group_check=True).then_inc(c.s_pe, 1)
            bump(c, "s_pe")
    bigblk.tensor(tb)
    c.c_stats_pe_end = c.s_pe_n


def emit_ln_rows_early(c, bigblk):
    rows = c.rows
    mrow = rows[0:1, 0:512]
    vrow = rows[0:1, 512:1024]
    msq = rows[0:1, 1024:1536]
    arow = rows[0:1, 1536:2048]
    brow = rows[0:1, 2048:2560]
    stats_pe = c.c_stats_pe_end
    dv0 = c.s_dv_n
    ac0 = c.s_ac_n

    def vbody(v):
        vp = EngProxy(c, v, "s_dv")
        v.wait_ge(c.s_pe, stats_pe)
        vp.tensor_scalar(mrow, c.psC[0:1, :], 1.0 / D, None, ALU.mult)
        vp.tensor_tensor(msq, mrow, mrow, ALU.mult)
        vp.tensor_scalar(vrow, c.psC[32:33, :], 1.0 / D, EPS, ALU.mult, ALU.add)
        vp.tensor_tensor(vrow, vrow, msq, ALU.subtract)
        v.wait_ge(c.s_ac, ac0 + 2)
        vp.scalar_tensor_tensor(brow, mrow, -1.0, arow, ALU.mult, ALU.mult)
    bigblk.vector(vbody)

    def sbody(s):
        sp = EngProxy(c, s, "s_ac")
        s.wait_ge(c.s_dv, dv0 + 4)
        sp.activation(arow, vrow, AF.Ln)
        sp.activation(arow, arow, AF.Exp, scale=-0.5)
    bigblk.scalar(sbody)
    c.rows_dv_done = dv0 + 5


def emit_ln_bcast_early(c, bigblk):
    arow = c.rows[0:1, 1536:2048]
    brow = c.rows[0:1, 2048:2560]
    dv_rows = c.rows_dv_done
    cb_dv = c.last_cb_dv
    pe_a = c.s_pe_n + 1
    pe_b = c.s_pe_n + 2
    ac_a = c.s_ac_n + 1

    def tbody(t):
        t.wait_ge(c.s_dv, max(dv_rows, cb_dv))
        t.matmul(c.psD[:, :], c.ones_row[0:1, :], arow, start=True, stop=True
                 ).then_inc(c.s_pe, 1)
        t.wait_ge(c.s_ac, ac_a)
        t.matmul(c.psD[:, :], c.ones_row[0:1, :], brow, start=True, stop=True
                 ).then_inc(c.s_pe, 1)
    bigblk.tensor(tbody)
    bump(c, "s_pe", 2)

    def sbody(s):
        sp = EngProxy(c, s, "s_ac")
        s.wait_ge(c.s_pe, pe_a)
        sp.activation(c.aB_sb[:, :], c.psD[:, :], AF.Copy)
        s.wait_ge(c.s_pe, pe_b)
        sp.activation(c.bB_sb[:, :], c.psD[:, :], AF.Copy)
    bigblk.scalar(sbody)
    c.bcast_ac_done = c.s_ac_n




def emit_phase_a(c, bigblk, blki):
    nc = c.nc
    scol = blki * BLK

    bk = bigblk
    if True:
        x_base = c.s_x_n
        if blki == 0:
            @bk.sync
            def _(sy):
                for j in range(4):
                    sy.dma_start(
                        c.xblk[:, :].rearrange("p (c s) -> p c s", c=NCH)[:, 4 * j:4 * j + 4, :],
                        c.xT.rearrange("(c p) s -> p c s", p=P)[:, 4 * j:4 * j + 4, scol:scol + BLK]
                    ).then_inc(c.s_x, 16)
            bump(c, "s_x", 64)
        else:
            dv0p = c.a_ln_dv0

            @bk.sync
            def _(sy):
                for ch in range(NCH):
                    sy.wait_ge(c.s_dv, dv0p + 23 + 2 * ch)
                    if ch >= 1:
                        sy.wait_ge(c.s_x, x_base + 16 * ch)
                    sy.dma_start(
                        c.xblk[:, :].rearrange("p (c s) -> p c s", c=NCH)[:, ch:ch + 1, :],
                        c.xT.rearrange("(c p) s -> p c s", p=P)[:, ch:ch + 1, scol:scol + BLK]
                    ).then_inc(c.s_x, 16)
            bump(c, "s_x", 256)
        x_ready = c.s_x_n

        if blki == 0:
            sc_done = c.s_sc_n

            @bk.gpsimd
            def _(g):
                g.wait_ge(c.s_sc, sc_done)
                for j in range(4):
                    g.dma_start(
                        c.wkT_sb[:, :].rearrange("p (c o) -> p c o", c=NCH)[:, 4 * j:4 * j + 4, :],
                        c.wkT.rearrange("(c p) o -> p c o", p=P)[:, 4 * j:4 * j + 4, :]
                    ).then_inc(c.s_wm, 16)
                for j in range(4):
                    g.dma_start(
                        c.wvT_sb[:, :].rearrange("p (c o) -> p c o", c=NCH)[:, 4 * j:4 * j + 4, :],
                        c.wvT.rearrange("(c p) o -> p c o", p=P)[:, 4 * j:4 * j + 4, :]
                    ).then_inc(c.s_wm, 16)
            bump(c, "s_wm", 128)
        wv_ready = c.s_wm_n
        sc_prev = c.s_sc_n

        c.a_ln_dv0 = c.s_dv_n
        if blki == 0:
            cwv = lambda v, ch: v.wait_ge(c.s_x, x_ready)
            cws = lambda s, ch: s.wait_ge(c.s_x, x_ready)
        else:
            cwv = lambda v, ch: v.wait_ge(c.s_x, x_base + 16 * (ch + 1))
            cws = lambda s, ch: s.wait_ge(c.s_x, x_base + 16 * (ch + 1))
        vb, sb, tb = make_ln_bodies(
            c, c.xblk, c.xln1T, (c.tbf0, c.tbf1, c.tsq0, c.tsq1, c.tmpA),
            chunk_wait_v=cwv,
            chunk_wait_s=cws,
            dst_war_wait=lambda v: v.wait_ge(c.s_sc, sc_prev),
        )
        bk.vector(vb)
        bk.scalar(sb)
        bk.tensor(tb)
    xln_done = c.s_dv_n

    bk2 = bigblk
    if True:
        pe0 = c.s_pe_n

        @bk2.tensor
        def _(t):
            t.wait_ge(c.s_dv, xln_done)
            t.wait_ge(c.s_wm, wv_ready)
            for dkc in range(4):
                ps = (c.psA[:, 0:512], c.psA[:, 512:1024])[dkc % 2]
                if dkc >= 2:
                    t.wait_ge(c.s_dv, xln_done + dkc - 1)
                for kc in range(NCH):
                    ins = t.matmul(ps, c.wkT_sb[:, 512 * kc + 128 * dkc:512 * kc + 128 * (dkc + 1)],
                                   c.xln1T[:, 512 * kc:512 * (kc + 1)],
                                   start=(kc == 0), stop=(kc == NCH - 1))
                ins.then_inc(c.s_pe, 1)
                bump(c, "s_pe")
            for tc4 in range(4):
                ps = (c.psA[:, 1024:1536], c.psB[:, 0:512])[tc4 % 2]
                if tc4 >= 2:
                    t.wait_ge(c.s_dv, xln_done + 4 + tc4 - 1)
                for kc in range(NCH):
                    ins = t.matmul(ps, c.xln1T[:, 512 * kc + 128 * tc4:512 * kc + 128 * (tc4 + 1)],
                                   c.wvT_sb[:, 512 * kc:512 * (kc + 1)],
                                   start=(kc == 0), stop=(kc == NCH - 1))
                ins.then_inc(c.s_pe, 1)
                bump(c, "s_pe")

        @bk2.vector
        def _(v):
            vp = EngProxy(c, v, "s_dv")
            for dkc in range(4):
                ps = (c.psA[:, 0:512], c.psA[:, 512:1024])[dkc % 2]
                v.wait_ge(c.s_pe, pe0 + dkc + 1)
                vp.tensor_copy(c.kT[:, S * dkc + scol:S * dkc + scol + BLK], ps)
            for tc4 in range(4):
                ps = (c.psA[:, 1024:1536], c.psB[:, 0:512])[tc4 % 2]
                tc = 4 * blki + tc4
                v.wait_ge(c.s_pe, pe0 + 4 + tc4 + 1)
                vp.tensor_copy(
                    c.v_aug[:, 640 * tc:640 * (tc + 1)].rearrange("p (h e) -> p h e", h=NH)[:, :, 0:DH],
                    ps.rearrange("p (h e) -> p h e", e=DH)
                )

        @bk2.gpsimd
        def _(g):
            g.wait_ge(c.s_dv, xln_done)
            for j in range(4):
                g.dma_start(
                    c.xln1T_scr[:, 4 * j:4 * j + 4, scol:scol + BLK],
                    c.xln1T[:, :].rearrange("p (c s) -> p c s", c=NCH)[:, 4 * j:4 * j + 4, :]
                ).then_inc(c.s_sc, 16)
        bump(c, "s_sc", 64)
    nconv = 16
    emit_convert_batch(c, bigblk, c.conv_cursor, c.conv_cursor + nconv)
    c.conv_cursor += nconv



def emit_phase_b(c, bigblk, blki):
    nc = c.nc
    scol = blki * BLK

    bk = bigblk
    if True:
        pe_prev = c.s_pe_n
        sc_prev = c.s_sc_n
        dv_prev = c.s_dv_n
        acc_prev = c.s_acc_n

        if blki == 0:
            @bk.sync
            def _(sy):
                sy.wait_ge(c.s_pe, pe_prev)
                sy.wait_ge(c.s_sc, sc_prev)
                sy.wait_ge(c.s_dv, dv_prev)
                sy.wait_ge(c.s_acc, acc_prev)
                for j in range(4):
                    sy.dma_start(
                        c.xblk[:, :].rearrange("p (c s) -> p c s", c=NCH)[:, 4 * j:4 * j + 4, :],
                        c.xT.rearrange("(c p) s -> p c s", p=P)[:, 4 * j:4 * j + 4, scol:scol + BLK]
                    ).then_inc(c.s_x, 16)
                for j in range(4):
                    sy.dma_start(
                        c.xln1T[:, :].rearrange("p (c s) -> p c s", c=NCH)[:, 4 * j:4 * j + 4, :],
                        c.xln1T_scr[:, 4 * j:4 * j + 4, scol:scol + BLK]
                    ).then_inc(c.s_x, 16)
            bump(c, "s_x", 128)
            xln1T_ready = ("x", c.s_x_n)
        else:
            q_pe_prev = c.q_pe_end
            ob_base = c.s_ob_n[0]

            @bk.sync
            def _(sy):
                sy.wait_ge(c.s_pe, q_pe_prev)
                sy.wait_ge(c.s_ob[0], ob_base)
                for j in range(4):
                    sy.dma_start(
                        c.xln1T[:, :].rearrange("p (c s) -> p c s", c=NCH)[:, 4 * j:4 * j + 4, :],
                        c.xln1T_scr[:, 4 * j:4 * j + 4, scol:scol + BLK]
                    ).then_inc(c.s_ob[0], 16)
            c.s_ob_n[0] += 64
            xln1T_ready = ("ob", c.s_ob_n[0])
            sc_base = c.prev_sc_store_base

            @bk.sync
            def _(sy):
                for ch in range(NCH):
                    sy.wait_ge(c.s_sc, sc_base + 16 * (ch + 1))
                    sy.dma_start(
                        c.xblk[:, :].rearrange("p (c s) -> p c s", c=NCH)[:, ch:ch + 1, :],
                        c.xT.rearrange("(c p) s -> p c s", p=P)[:, ch:ch + 1, scol:scol + BLK]
                    ).then_inc(c.s_x, 16)
            bump(c, "s_x", 256)
        x_full = c.s_x_n
        c.b_x_full = x_full

        wq0_0 = c.s_wq0_n
        wq1_0 = c.s_wq1_n
        pe0 = c.s_pe_n
        dv0 = c.s_dv_n

        @bk.sync
        def _(sy2):
            for half in range(2):
                for kc in range(NCH):
                    idx = half * NCH + kc
                    buf = (c.wq_kc0, c.wq_kc1)[idx % 2]
                    sem = (c.s_wq0, c.s_wq1)[idx % 2]
                    if idx >= 2:
                        sy2.wait_ge(c.s_pe, pe0 + idx - 1)
                    sy2.dma_start(buf[:, :], c.wq_scr[:, kc, :]).then_inc(sem, 16)
        bump(c, "s_wq0", 256)
        bump(c, "s_wq1", 256)

        @bk.tensor
        def _(t):
            slices = [c.psA[:, 0:512], c.psA[:, 512:1024], c.psA[:, 1024:1536],
                      c.psB[:, 0:512], c.psB[:, 512:1024], c.psB[:, 1024:1536],
                      c.psC[:, :], c.psD[:, :]]
            if xln1T_ready[0] == "x":
                t.wait_ge(c.s_x, xln1T_ready[1])
            else:
                t.wait_ge(c.s_ob[0], xln1T_ready[1])
            t.wait_ge(c.s_dv, dv0)
            t.wait_ge(c.s_acc, acc_prev)
            for half in range(2):
                if half == 1:
                    t.wait_ge(c.s_dv, dv0 + 8)
                for kc in range(NCH):
                    idx = half * NCH + kc
                    buf = (c.wq_kc0, c.wq_kc1)[idx % 2]
                    t.wait_ge((c.s_wq0, c.s_wq1)[idx % 2],
                              (wq0_0, wq1_0)[idx % 2] + 16 * (idx // 2 + 1))
                    for j in range(8):
                        oc = half * 8 + j
                        ins = t.matmul(slices[j], buf[:, 128 * oc:128 * (oc + 1)],
                                       c.xln1T[:, 512 * kc:512 * (kc + 1)],
                                       start=(kc == 0), stop=(kc == NCH - 1),
                                       skip_group_check=True)
                    ins.then_inc(c.s_pe, 1)
                    bump(c, "s_pe")
        c.q_pe_end = c.s_pe_n

        @bk.vector
        def _(v):
            vp = EngProxy(c, v, "s_dv")
            slices = [c.psA[:, 0:512], c.psA[:, 512:1024], c.psA[:, 1024:1536],
                      c.psB[:, 0:512], c.psB[:, 512:1024], c.psB[:, 1024:1536],
                      c.psC[:, :], c.psD[:, :]]
            for half in range(2):
                v.wait_ge(c.s_pe, pe0 + NCH * (half + 1))
                for j in range(8):
                    oc = half * 8 + j
                    vp.tensor_copy(c.qT[:, 512 * oc:512 * (oc + 1)], slices[j])

    bk2 = bigblk
    if True:
        dv0 = c.s_dv_n
        ac0 = c.s_ac_n

        @bk2.vector
        def _(v):
            vp = EngProxy(c, v, "s_dv")
            for gpi in range(8):
                sl = slice(1024 * gpi, 1024 * (gpi + 1))
                if gpi < 4:
                    v.wait_ge(c.s_dv, dv0 - 16 + 2 * gpi + 2)
                vp.tensor_scalar_min(c.et1[:, :], c.qT[:, sl], 0.0)
                vp.tensor_scalar_max(c.et2[:, :], c.qT[:, sl], 0.0)
                v.wait_ge(c.s_ac, ac0 + gpi + 1)
                vp.scalar_tensor_tensor(c.sigmaT[:, sl], c.et1[:, :], -1.0, c.et2[:, :],
                                        ALU.add, ALU.add)

        @bk2.scalar
        def _(s):
            sp = EngProxy(c, s, "s_ac")
            for gpi in range(8):
                s.wait_ge(c.s_dv, dv0 + 3 * gpi + 1)
                sp.activation(c.et1[:, :], c.et1[:, :], AF.Exp)

    for qg in range(NQG):
        _emit_amem(c, bigblk, qg)

    c.cur_scol = scol
    c.prev_sc_store_base = c.s_sc_n
    _emit_pairs_pipelined(c, bigblk)

    bst = bigblk
    if True:
        dvv = c.s_dv_n
        accv = c.s_acc_n

        @bst.gpsimd
        def _(g):
            g.wait_ge(c.s_dv, dvv)
            g.wait_ge(c.s_acc, accv)
            g.wait_ge(c.s_sc, c.s_sc_n)
            g.dma_start(c.x2_scr[:, 15, scol:scol + BLK],
                        c.xblk[:, 512 * 15:512 * 16]).then_inc(c.s_sc, 16)
        bump(c, "s_sc", 16)
    if blki == 3:
        nconv = len(c.conv_chunks) - c.conv_cursor
        emit_convert_batch(c, bigblk, c.conv_cursor, c.conv_cursor + nconv)
        c.conv_cursor += nconv
    if "x2" in c.taps:
        _tap(c, bigblk, f"x2_{blki}", c.xblk, [P, NCH * 512], F32)
    if "qT" in c.taps and blki == 0:
        _tap(c, bigblk, f"qT_{blki}", c.qT, [P, NCH * 512], BF16)
    if "sigmaT" in c.taps and blki == 0:
        _tap(c, bigblk, f"sigmaT_{blki}", c.sigmaT, [P, NCH * 512], BF16)
    if "kT" in c.taps and blki == 0:
        _tap(c, bigblk, f"kT_{blki}", c.kT, [P, 4 * S], BF16)
    if "vaug" in c.taps and blki == 0:
        _tap(c, bigblk, f"vaug_{blki}", c.v_aug, [P, NCH * 520], BF16)


def _emit_amem(c, bigblk, qg):
    nc = c.nc
    rd = c.rows[0:1, 2560:3072]
    bk = bigblk
    if True:
        dv0 = c.s_dv_n
        pe0 = c.s_pe_n

        @bk.tensor
        def _(t):
            t.wait_ge(c.s_dv, dv0)
            for dkoc in range(4):
                ps = (c.psA[:, 0:512], c.psA[:, 512:1024], c.psA[:, 1024:1536],
                      c.psB[:, 0:512])[dkoc]
                for ci in range(4):
                    ins = t.matmul(ps, c.mem_sb[:, 512 * ci + 128 * dkoc:512 * ci + 128 * (dkoc + 1)],
                                   c.sigmaT[:, 512 * (4 * qg + ci):512 * (4 * qg + ci + 1)],
                                   start=(ci == 0), stop=(ci == 3))
            ins.then_inc(c.s_pe, 1)
            bump(c, "s_pe")
            for ci in range(4):
                ins = t.matmul(c.psB[0:1, 512:1024], c.mn_sb[:, ci:ci + 1],
                               c.sigmaT[:, 512 * (4 * qg + ci):512 * (4 * qg + ci + 1)],
                               start=(ci == 0), stop=(ci == 3))
            ins.then_inc(c.s_pe, 1)
            bump(c, "s_pe")
            t.wait_ge(c.s_dv, dv0 + 2)
            t.matmul(c.psB[:, 1024:1536], c.ones_row[0:1, :], rd, start=True, stop=True
                     ).then_inc(c.s_pe, 1)
            bump(c, "s_pe")

        @bk.vector
        def _(v):
            vp = EngProxy(c, v, "s_dv")
            v.wait_ge(c.s_x, c.b_x_full)
            v.wait_ge(c.s_pe, pe0 + 2)
            vp.reciprocal(rd, c.psB[0:1, 512:1024])
            vp.tensor_scalar(rd, rd, c.w_ap[0:1, 0:1], None, ALU.mult)
            v.wait_ge(c.s_pe, pe0 + 3)
            vp.tensor_copy(c.amB[:, :], c.psB[:, 1024:1536])
            for dkoc in range(4):
                ps = (c.psA[:, 0:512], c.psA[:, 512:1024], c.psA[:, 1024:1536],
                      c.psB[:, 0:512])[dkoc]
                ch = 4 * qg + dkoc
                vp.tensor_tensor(c.tmpA[:, :], ps, c.amB[:, :], ALU.mult)
                vp.tensor_tensor(c.xblk[:, 512 * ch:512 * (ch + 1)],
                                 c.xblk[:, 512 * ch:512 * (ch + 1)], c.tmpA[:, :],
                                 ALU.add)


def _emit_pairs_pipelined(c, bigblk):
    ac_base = c.s_ac_n
    dv_amem = c.s_dv_n
    states = []
    for n in range(17):
        cur = None
        if n < 16:
            qg, j = n // 4, n % 4
            cur = dict(n=n, oc=n, h_e=2 * j, h_o=2 * j + 1,
                       pbuf_e=(c.probs_e0, c.probs_e1)[n % 2],
                       pbuf_o=(c.probs_o0, c.probs_o1)[n % 2],
                       sc_e={}, sc_o={}, exp_e={}, exp_o={})
            base = ac_base + 16 * n
            for tp in range(8):
                cur["exp_e"][tp] = base + 2 * tp + 1
                cur["exp_o"][tp] = base + 2 * tp + 2
        prev = states[n - 1] if n >= 1 else None
        prev2 = states[n - 2] if n >= 2 else None
        _emit_slot(c, bigblk, cur, prev, prev2, dv_amem)
        states.append(cur)
        nconv = min(6, len(c.conv_chunks) - c.conv_cursor)
        if nconv:
            emit_convert_batch(c, bigblk, c.conv_cursor, c.conv_cursor + nconv)
            c.conv_cursor += nconv
    assert c.s_ac_n == ac_base + 256, c.s_ac_n - ac_base


def _pv_batch(c, t, prev, prev2, parity, lo, hi):
    pbuf = prev["pbuf_" + parity]
    h = prev["h_" + parity]
    bank = c.psC if parity == "e" else c.psD
    if lo == 0:
        t.wait_ge(c.s_ac, prev["exp_" + parity][7])
        if prev2 is not None:
            t.wait_ge(c.s_dv, prev2["tmp2_" + parity])
    for tc in range(lo, hi):
        ins = t.matmul(bank[:, :],
                       c.v_aug[:, 640 * tc + 80 * h:640 * tc + 80 * h + 128],
                       pbuf[:, 512 * tc:512 * (tc + 1)],
                       start=(tc == 0), stop=(tc == 15), skip_group_check=True)
    ins.then_inc(c.s_pe, 1)
    bump(c, "s_pe")
    if hi == 16:
        prev["pv_" + parity + "_done"] = c.s_pe_n


def _emit_slot(c, bigblk, cur, prev, prev2, dv_amem):
    nc = c.nc
    dv_base = c.s_dv_n
    if prev is not None:
        prev["zcopy_e"] = dv_base + 1
        prev["tmp2_e"] = dv_base + 3
        prev["x2e"] = dv_base + 4
        prev["zcopy_o"] = dv_base + 5
        prev["tmp2_o"] = dv_base + 7
    rz_e = c.rows[0:1, 3072:3584]
    rz_o = c.rows[0:1, 3584:4096]

    def tb(t):
        for tp in range(8):
            if prev is not None:
                if tp < 4:
                    _pv_batch(c, t, prev, prev2, "e", 4 * tp, 4 * tp + 4)
                else:
                    _pv_batch(c, t, prev, prev2, "o", 4 * (tp - 4), 4 * (tp - 4) + 4)
            if prev is not None and tp == 4:
                t.wait_ge(c.s_dv, prev["zcopy_e"])
                t.matmul(c.psA[0:DH, 1024:1536], c.ones_row[0:1, 0:DH], rz_e,
                         start=True, stop=True).then_inc(c.s_pe, 1)
                prev["bcast_e"] = bump(c, "s_pe")
            if cur is not None:
                if tp == 0:
                    t.wait_ge(c.s_dv, dv_amem)
                    if prev is not None:
                        t.wait_ge(c.s_ac, prev["exp_e"][7])
                    if prev2 is not None:
                        t.wait_ge(c.s_dv, prev2["tmp2_o"])
                else:
                    t.wait_ge(c.s_ac, cur["exp_e"][tp - 1])
                    if tp == 1 and prev is not None:
                        t.wait_ge(c.s_ac, prev["exp_o"][7])
                kvc = cur["oc"] % 4
                qcol = 512 * cur["oc"]
                for i, (pb, ps) in enumerate((
                        (0, c.psA[:, 0:512]),
                        (DH, c.psB[:, 0:512]),
                        (0, c.psA[:, 512:1024]),
                        (DH, c.psB[:, 512:1024]))):
                    tc = 2 * tp + (i // 2)
                    if i == 1:
                        ow = cur["exp_o"][tp - 1] if tp >= 1 else (
                            prev["exp_o"][7] if prev is not None else None)
                        if ow is not None:
                            t.wait_ge(c.s_ac, ow)
                    ins = t.matmul(
                        ps,
                        c.kT[pb:pb + DH, S * kvc + 128 * tc:S * kvc + 128 * (tc + 1)],
                        c.qT[pb:pb + DH, qcol:qcol + 512],
                        start=True, stop=True, skip_group_check=True)
                    ins.then_inc(c.s_pe, 1)
                    bump(c, "s_pe")
                    if i == 2:
                        cur["sc_e"][tp] = c.s_pe_n
                    elif i == 3:
                        cur["sc_o"][tp] = c.s_pe_n
        if prev is not None:
            t.wait_ge(c.s_dv, prev["zcopy_o"])
            t.matmul(c.psB[0:DH, 1024:1536], c.ones_row[0:1, 0:DH], rz_o,
                     start=True, stop=True).then_inc(c.s_pe, 1)
            prev["bcast_o"] = bump(c, "s_pe")
    bigblk.tensor(tb)

    def sb(s):
        sp = EngProxy(c, s, "s_ac")
        for tp in range(8):
            if cur is not None:
                s.wait_ge(c.s_pe, cur["sc_e"][tp])
                if tp == 0 and prev2 is not None:
                    s.wait_ge(c.s_pe, prev2["pv_e_done"])
                    s.wait_ge(c.s_pe, prev2["pv_o_done"])
                sp.activation(cur["pbuf_e"][:, 1024 * tp:1024 * (tp + 1)],
                              c.psA[:, 0:1024], AF.Exp, scale=SCALE)
                s.wait_ge(c.s_pe, cur["sc_o"][tp])
                sp.activation(cur["pbuf_o"][:, 1024 * tp:1024 * (tp + 1)],
                              c.psB[:, 0:1024], AF.Exp, scale=SCALE)
    bigblk.scalar(sb)

    if prev is not None:
        oc = prev["oc"]

        def vb(v):
            vp = EngProxy(c, v, "s_dv")
            v.wait_ge(c.s_pe, prev["pv_e_done"])
            vp.tensor_copy(rz_e, c.psC[64:65, :])
            v.wait_ge(c.s_pe, prev["bcast_e"])
            vp.reciprocal(c.rzB_sb[0:DH, :], c.psA[0:DH, 1024:1536])
            vp.scalar_tensor_tensor(c.tmp2a[0:DH, :], c.psC[0:DH, :],
                                    c.w1m_cT[0:DH, 0:1], c.rzB_sb[0:DH, :],
                                    ALU.mult, ALU.mult)
            vp.tensor_tensor(c.xblk[0:DH, 512 * oc:512 * (oc + 1)],
                             c.xblk[0:DH, 512 * oc:512 * (oc + 1)],
                             c.tmp2a[0:DH, :], ALU.add)
            v.wait_ge(c.s_pe, prev["pv_o_done"])
            vp.tensor_copy(rz_o, c.psD[64:65, :])
            v.wait_ge(c.s_pe, prev["bcast_o"])
            if prev2 is not None and prev2.get("acc16"):
                v.wait_ge(c.s_acc, prev2["acc16"])
            vp.reciprocal(c.rzB_sb[0:DH, :], c.psB[0:DH, 1024:1536])
            vp.scalar_tensor_tensor(c.tmp2b[0:DH, :], c.psD[0:DH, :],
                                    c.w1m_cT[0:DH, 0:1], c.rzB_sb[0:DH, :],
                                    ALU.mult, ALU.mult)
        bigblk.vector(vb)
        assert c.s_dv_n == dv_base + 7

        def gb(g):
            g.wait_ge(c.s_dv, prev["tmp2_o"])
            g.dma_start(c.xblk[DH:P, 512 * oc:512 * (oc + 1)], c.tmp2b[0:DH, :],
                        accum_op=ALU.add).then_inc(c.s_acc, 16)
            if prev2 is not None:
                oc2 = prev2["oc"]
                g.wait_ge(c.s_dv, prev2["x2e"])
                g.wait_ge(c.s_sc, c.s_sc_n)
                g.dma_start(c.x2_scr[:, oc2, c.cur_scol:c.cur_scol + BLK],
                            c.xblk[:, 512 * oc2:512 * (oc2 + 1)]
                            ).then_inc(c.s_sc, 16)
        bigblk.gpsimd(gb)
        if prev2 is not None:
            bump(c, "s_sc", 16)
        prev["acc16"] = bump(c, "s_acc", 16)



def emit_phase_c(c, bigblk, st):
    nc = c.nc
    scol = st * BLK

    bk = bigblk
    if st == 0:
        dv_prev = c.s_dv_n
        gp_prev = c.s_gp_n
        sc_prev = c.s_sc_n
        cv_tot = c.s_cv_n
        load_vals = [c.s_ob_n[b] + 16 for b in range(4)]

        @bk.sync
        def _(sy):
            sy.wait_ge(c.s_dv, dv_prev)
            sy.wait_ge(c.s_gp, gp_prev)
            sy.wait_ge(c.s_sc, sc_prev)
            sy.wait_ge(c.s_cv, cv_tot)
            for b in range(4):
                sy.dma_start(
                    c.acc[:, :].rearrange("p (c s) -> p c s", c=NCH)[:, 4 * b:4 * b + 4, :],
                    c.x2_scr[:, 4 * b:4 * b + 4, scol:scol + BLK]
                ).then_inc(c.s_ob[b], 16)
        for b in range(4):
            c.s_ob_n[b] += 16

        def _prew(eng):
            eng.wait_ge(c.s_cv, cv_tot)

        def _chw(eng, ch):
            eng.wait_ge(c.s_ob[ch // 4], load_vals[ch // 4])

        vb, sb, tb = make_ln_bodies(
            c, c.acc, c.xln2T, (c.ctbf0, c.ctbf1, c.ctsq0, c.ctsq1, c.ctmp),
            pre_wait_v=_prew,
            pre_wait_s=_prew,
            chunk_wait_v=_chw,
            chunk_wait_s=_chw,
        )
        bk.vector(vb)
        bk.scalar(sb)
        bk.tensor(tb)
        ln_done = c.s_dv_n
        apply_gp_end = None
    else:
        store_vals = list(c.s_ob_n)
        load_vals = [v + 16 for v in store_vals]

        @bk.sync
        def _(sy):
            for b in range(4):
                sy.wait_ge(c.s_ob[b], store_vals[b])
                sy.dma_start(
                    c.acc[:, :].rearrange("p (c s) -> p c s", c=NCH)[:, 4 * b:4 * b + 4, :],
                    c.x2_scr[:, 4 * b:4 * b + 4, scol:scol + BLK]
                ).then_inc(c.s_ob[b], 16)
        for b in range(4):
            c.s_ob_n[b] += 16

        c.acc_load_vals = load_vals
        ln_done = None

    bk2 = bigblk
    if True:
        @bk2.tensor
        def _(t):
            if st == 0:
                t.wait_ge(c.s_dv, ln_done)
            else:
                t.wait_ge(c.s_dv, c.apply_dv_end)
            for s4 in range(4):
                for kc in range(NCH):
                    ins = t.matmul(c.psC[:, 8 * s4:8 * (s4 + 1)],
                                   c.xln2T[:, 512 * kc + 128 * s4:512 * kc + 128 * (s4 + 1)],
                                   c.routerT_sb[:, 8 * kc:8 * (kc + 1)],
                                   start=(kc == 0), stop=(kc == NCH - 1),
                                   skip_group_check=True)
            ins.then_inc(c.s_pe, 1)
        bump(c, "s_pe")
        pe_logits = c.s_pe_n

        @bk2.scalar
        def _(s):
            sp = EngProxy(c, s, "s_ac")
            s.wait_ge(c.s_pe, pe_logits)
            sp.activation(c.exp32[:, :], c.psC[:, 0:32], AF.Exp)
        ac_exp = c.s_ac_n
        c.top2_ac_exp = ac_exp
    comb_dv = 0

    if "combine" in c.taps:
        _tap(c, bigblk, f"combine_{st}", c.combine, [P, 32], F32)
    if "xln2T" in c.taps:
        _tap(c, bigblk, f"xln2T_{st}", c.xln2T, [P, NCH * 512], BF16)

    order = [8, 9] + list(range(8))
    for pos, e in enumerate(order):
        _emit_expert(c, bigblk, st, e, comb_dv,
                     last=(pos == NEXP - 1),
                     st_last=(st == NBLK - 1),
                     gate_preloaded=(pos == 0 and st > 0),
                     first=(pos == 0))
        if pos == 0:
            for s4 in range(4):
                _emit_top2(c, bigblk, s4, c.top2_ac_exp)
        if pos == 5 and st < 3:
            emit_c_stats_early(c, bigblk, st + 1)
            emit_ln_rows_early(c, bigblk)
        if pos == 6 and st < 3:
            emit_ln_bcast_early(c, bigblk)


def _emit_top2(c, bk, s4, ac_exp):
    m1 = c.r1a[:, 1:2]
    m2 = c.r1a[:, 2:3]
    r12 = c.r1a[:, 3:4]
    sl = slice(8 * s4, 8 * (s4 + 1))
    X = c.exp32[:, sl]

    def vbody2(v):
        vp = EngProxy(c, v, "s_dv")
        v.wait_ge(c.s_ac, ac_exp)
        vp.reduce_sum(m1, X, AX)
        vp.reciprocal(m2, m1)
        vp.tensor_scalar(X, X, m2, None, ALU.mult)
        vp.tensor_reduce(m1, X, AX, ALU.max)
        vp.tensor_scalar(c.r8a[:, :], X, m1, None, ALU.is_ge)
        vp.scalar_tensor_tensor(c.r8b[:, :], c.r8a[:, :], -1e30, X,
                                ALU.mult, ALU.add)
        vp.tensor_reduce(m2, c.r8b[:, :], AX, ALU.max)
        vp.tensor_scalar(c.r8a[:, :], X, m2, None, ALU.is_ge)
        vp.tensor_tensor(m1, m1, m2, ALU.add)
        vp.reciprocal(r12, m1)
        vp.tensor_tensor(c.r8b[:, :], X, c.r8a[:, :], ALU.mult)
        vp.tensor_scalar(c.combine[:, sl], c.r8b[:, :], r12, None, ALU.mult)
    bk.vector(vbody2)


def _emit_expert(c, bigblk, st, e, comb_dv, last=False, st_last=False,
                 gate_preloaded=False, first=False):
    nc = c.nc
    shared = e >= 8
    half = e - 8
    par = e % 2
    sl_g = (c.slot0, c.slot1)[par]
    sl_u = (c.slot0, c.slot1)[1 - par]
    sl_d = sl_g

    gsrc = c.gu_scr[:, 32 * e:32 * e + 16, :]
    usrc = c.gu_scr[:, 32 * e + 16:32 * e + 32, :]
    dsrc = c.dn_scr[:, 11 * e:11 * e + 11, :]

    bk = bigblk
    if True:
        if gate_preloaded:
            gate_w = c.pre_gate_w
        else:
            gate_slot_free = c.u_pe_end
            cv_done = c.s_cv_n

            @bk.sync
            def _(sy):
                sy.wait_ge(c.s_cv, cv_done)
                sy.wait_ge(c.s_pe, gate_slot_free)
                for j in range(4):
                    sy.dma_start(
                        sl_g[:, :].rearrange("p (ch i) -> p ch i", i=IFF)[:, 4 * j:4 * j + 4, :],
                        gsrc[:, 4 * j:4 * j + 4, :]
                    ).then_inc(c.s_wg, 16)
            bump(c, "s_wg", 64)
            gate_w = c.s_wg_n

        if not shared:
            dv_snap = c.s_dv_n
            ac_cb_snap = c.bcast_ac_done

            @bk.tensor
            def _(t):
                t.wait_ge(c.s_dv, dv_snap)
                t.wait_ge(c.s_ac, ac_cb_snap)
                for s4 in range(4):
                    ins = t.transpose(c.psD[0:1, 128 * s4:128 * (s4 + 1)],
                                      c.combine[:, 8 * s4 + e:8 * s4 + e + 1],
                                      c.ident[:, :])
                ins.then_inc(c.s_pe, 1)
            bump(c, "s_pe")
            pe_tr = c.s_pe_n

            @bk.vector
            def _(v):
                vp = EngProxy(c, v, "s_dv")
                v.wait_ge(c.s_pe, pe_tr)
                vp.tensor_copy(c.cb_row[0:1, :], c.psD[0:1, :])
            dv_cb = c.s_dv_n

            @bk.tensor
            def _(t2):
                t2.wait_ge(c.s_dv, dv_cb)
                t2.matmul(c.psD[:, :], c.ones_row[0:1, :], c.cb_row[0:1, :],
                          start=True, stop=True).then_inc(c.s_pe, 1)
            bump(c, "s_pe")
            pe_cbB = c.s_pe_n

            @bk.vector
            def _(v2):
                vp = EngProxy(c, v2, "s_dv")
                v2.wait_ge(c.s_pe, pe_cbB)
                vp.tensor_copy(c.cbB[:, :], c.psD[:, :])
            c.last_cb_dv = c.s_dv_n
    cb_dv = c.s_dv_n

    bk2 = bigblk
    if True:
        pe0 = c.s_pe_n
        ac0 = c.s_ac_n
        dv_snap = c.s_dv_n
        ac_snap = c.s_ac_n
        gp_snap = c.s_gp_n

        @bk2.tensor
        def _(t):
            pings = (c.psA[:, 0:512], c.psA[:, 512:1024])
            t.wait_ge(c.s_dv, dv_snap)
            t.wait_ge(c.s_ac, ac_snap)
            t.wait_ge(c.s_gp, gp_snap)
            t.wait_ge(c.s_wg, gate_w)
            for ic in range(ICH):
                if ic >= 2:
                    t.wait_ge(c.s_ac, ac0 + ic - 1)
                for kc in range(NCH):
                    ins = t.matmul(pings[ic % 2],
                                   sl_g[:, IFF * kc + 128 * ic:IFF * kc + 128 * (ic + 1)],
                                   c.xln2T[:, 512 * kc:512 * (kc + 1)],
                                   start=(kc == 0), stop=(kc == NCH - 1))
                ins.then_inc(c.s_pe, 1)
                bump(c, "s_pe")

        @bk2.scalar
        def _(s):
            sp = EngProxy(c, s, "s_ac")
            pings = (c.psA[:, 0:512], c.psA[:, 512:1024])
            for ic in range(ICH):
                s.wait_ge(c.s_pe, pe0 + ic + 1)
                if shared:
                    sp.activation(c.silu_g[:, 512 * ic:512 * (ic + 1)], pings[ic % 2],
                                  AF.Silu,
                                  bias=c.gb_col[:, 11 * half + ic:11 * half + ic + 1])
                else:
                    sp.activation(c.silu_g[:, 512 * ic:512 * (ic + 1)], pings[ic % 2],
                                  AF.Silu)

        @bk2.sync
        def _(sy):
            sy.wait_ge(c.s_pe, pe0)
            for j in range(4):
                sy.dma_start(
                    sl_u[:, :].rearrange("p (ch i) -> p ch i", i=IFF)[:, 4 * j:4 * j + 4, :],
                    usrc[:, 4 * j:4 * j + 4, :]
                ).then_inc(c.s_wu, 16)
        bump(c, "s_wu", 64)
    up_w = c.s_wu_n
    g_pe_end = c.s_pe_n
    ac_g_end = c.s_ac_n

    bk3 = bigblk
    if True:
        pe0 = c.s_pe_n
        dv0 = c.s_dv_n

        @bk3.tensor
        def _(t):
            pings = (c.psA[:, 1024:1536], c.psB[:, 0:512])
            t.wait_ge(c.s_wu, up_w)
            t.wait_ge(c.s_dv, dv0)
            for ic in range(ICH):
                if ic >= 2:
                    t.wait_ge(c.s_dv, dv0 + ic - 1)
                for kc in range(NCH):
                    ins = t.matmul(pings[ic % 2],
                                   sl_u[:, IFF * kc + 128 * ic:IFF * kc + 128 * (ic + 1)],
                                   c.xln2T[:, 512 * kc:512 * (kc + 1)],
                                   start=(kc == 0), stop=(kc == NCH - 1))
                ins.then_inc(c.s_pe, 1)
                bump(c, "s_pe")
        c.u_pe_end = c.s_pe_n

        @bk3.sync
        def _(sy):
            sy.wait_ge(c.s_pe, g_pe_end)
            for j in range(4):
                sy.dma_start(
                    sl_d[:, :].rearrange("p (ch o) -> p ch o", o=D)[:, 3 * j:min(3 * j + 3, ICH), :],
                    dsrc[:, 3 * j:min(3 * j + 3, ICH), :]
                ).then_inc(c.s_wd, 16)
        bump(c, "s_wd", 64)
        down_w = c.s_wd_n

        if shared:
            @bk3.vector
            def _(v):
                vp = EngProxy(c, v, "s_dv")
                pings = (c.psA[:, 1024:1536], c.psB[:, 0:512])
                v.wait_ge(c.s_ac, ac_g_end)
                for ic in range(ICH):
                    v.wait_ge(c.s_pe, pe0 + ic + 1)
                    vp.scalar_tensor_tensor(c.h_sb[:, 512 * ic:512 * (ic + 1)],
                                            pings[ic % 2],
                                            c.ub_col[:, 11 * half + ic:11 * half + ic + 1],
                                            c.silu_g[:, 512 * ic:512 * (ic + 1)],
                                            ALU.add, ALU.mult)
        else:
            @bk3.vector
            def _(v):
                vp = EngProxy(c, v, "s_dv")
                pings = (c.psA[:, 1024:1536], c.psB[:, 0:512])
                v.wait_ge(c.s_ac, ac_g_end)
                for ic in range(ICH):
                    v.wait_ge(c.s_pe, pe0 + ic + 1)
                    vp.tensor_tensor(c.h_sb[:, 512 * ic:512 * (ic + 1)],
                                     c.silu_g[:, 512 * ic:512 * (ic + 1)],
                                     pings[ic % 2], ALU.mult)

            @bk3.gpsimd
            def _(g):
                gp = EngProxy(c, g, "s_gp")
                g.wait_ge(c.s_dv, cb_dv)
                for ic in range(ICH):
                    g.wait_ge(c.s_dv, dv0 + ic + 1)
                    gp.tensor_tensor(c.h_sb[:, 512 * ic:512 * (ic + 1)],
                                     c.h_sb[:, 512 * ic:512 * (ic + 1)],
                                     c.cbB[:, :], ALU.mult)
    h_dv = c.s_dv_n
    h_gp = c.s_gp_n

    if last and not st_last:
        cv_done2 = c.s_cv_n
        upe = c.u_pe_end
        ngsrc = c.gu_scr[:, 32 * 8:32 * 8 + 16, :]

        @bigblk.sync
        def _(sy):
            sy.wait_ge(c.s_cv, cv_done2)
            sy.wait_ge(c.s_pe, upe)
            for j in range(4):
                sy.dma_start(
                    c.slot0[:, :].rearrange("p (ch i) -> p ch i", i=IFF)[:, 4 * j:4 * j + 4, :],
                    ngsrc[:, 4 * j:4 * j + 4, :]
                ).then_inc(c.s_wg, 16)
        bump(c, "s_wg", 64)
        c.pre_gate_w = c.s_wg_n

        dv0_pred = c.s_dv_n
        xs_base = [c.s_xs0_n, c.s_xs1_n]
        scol_n = (st + 1) * BLK

        @bigblk.sync
        def _(sy):
            for ch in range(NCH):
                p = ch % 2
                sem = (c.s_xs0, c.s_xs1)[p]
                if ch >= 2:
                    sy.wait_ge(sem, xs_base[p] + 16 * (ch // 2))
                if ch >= 4:
                    sy.wait_ge(c.s_dv, dv0_pred + 3 * (ch - 4) + 3)
                sy.dma_start(c.xs_bufs[ch % 4][:, :],
                             c.x2_scr[:, ch, scol_n:scol_n + BLK]
                             ).then_inc(sem, 16)
        bump(c, "s_xs0", 128)
        bump(c, "s_xs1", 128)
        c.stream_dv0 = dv0_pred
        c.stream_xs_base = xs_base

    bk4 = bigblk
    if True:
        pe0 = c.s_pe_n
        dv0 = c.s_dv_n
        stream = last and not st_last
        nops = 3 if stream else 1
        if stream:
            assert dv0 == c.stream_dv0

        @bk4.tensor
        def _(t):
            pings = (c.psB[:, 512:1024], c.psB[:, 1024:1536])
            t.wait_ge(c.s_wd, down_w)
            t.wait_ge(c.s_dv, h_dv)
            if not shared:
                t.wait_ge(c.s_gp, h_gp)
            for och in range(NCH):
                if och >= 2:
                    t.wait_ge(c.s_dv, dv0 + nops * (och - 2) + 1)
                for ic in range(ICH):
                    ins = t.matmul(pings[och % 2],
                                   sl_d[:, D * ic + 128 * och:D * ic + 128 * (och + 1)],
                                   c.h_sb[:, 512 * ic:512 * (ic + 1)],
                                   start=(ic == 0), stop=(ic == ICH - 1))
                ins.then_inc(c.s_pe, 1)
                bump(c, "s_pe")

        @bk4.vector
        def _(v):
            vp = EngProxy(c, v, "s_dv")
            pings = (c.psB[:, 512:1024], c.psB[:, 1024:1536])
            for och in range(NCH):
                v.wait_ge(c.s_pe, pe0 + och + 1)
                if first and st > 0:
                    v.wait_ge(c.s_ob[och // 4], c.acc_load_vals[och // 4])
                if last:
                    vp.scalar_tensor_tensor(c.acc[:, 512 * och:512 * (och + 1)],
                                            pings[och % 2],
                                            c.db_col[:, och:och + 1],
                                            c.acc[:, 512 * och:512 * (och + 1)],
                                            ALU.add, ALU.add)
                else:
                    vp.tensor_tensor(c.acc[:, 512 * och:512 * (och + 1)], pings[och % 2],
                                     c.acc[:, 512 * och:512 * (och + 1)], ALU.add)
                if stream:
                    if och == 0:
                        v.wait_ge(c.s_ac, c.bcast_ac_done)
                    p = och % 2
                    v.wait_ge((c.s_xs0, c.s_xs1)[p],
                              c.stream_xs_base[p] + 16 * (och // 2 + 1))
                    vp.tensor_tensor(c.ctmp[:, :], c.xs_bufs[och % 4][:, :],
                                     c.aB_sb[:, :], ALU.mult)
                    vp.tensor_tensor(c.xln2T[:, 512 * och:512 * (och + 1)],
                                     c.ctmp[:, :], c.bB_sb[:, :], ALU.add)
        if stream:
            c.apply_dv_end = dv0 + 3 * NCH

        if last:
            scol = st * BLK
            lane_vals = list(c.s_ob_n)

            @bk4.sync
            def _(sy):
                for b in range(4):
                    sy.wait_ge(c.s_dv, dv0 + nops * (4 * b + 3) + 1)
                    sy.wait_ge(c.s_ob[b], lane_vals[b])
                    sy.dma_start(
                        c.outT.rearrange("(c p) s -> p c s", p=P)[:, 4 * b:4 * b + 4, scol:scol + BLK],
                        c.acc[:, :].rearrange("p (c s) -> p c s", c=NCH)[:, 4 * b:4 * b + 4, :]
                    ).then_inc(c.s_ob[b], 16)
            for b in range(4):
                c.s_ob_n[b] += 16


def emit_epilogue(c, bigblk):
    nc = c.nc
    o_total = c.s_o_n
    lane_vals = list(c.s_ob_n)
    bk = bigblk
    if True:
        @bk.sync
        def _(sy):
            sy.wait_ge(c.s_o, o_total)
            for b in range(4):
                sy.wait_ge(c.s_ob[b], lane_vals[b])


def _tap(c, bigblk, name, sb, shape, dtype):
    nc = c.nc
    t = nc.dram_tensor("tap_" + name, list(shape), dtype, kind="ExternalOutput").ap()
    bk = bigblk
    if True:
        dvv = c.s_dv_n
        scv = c.s_sc_n
        accv = c.s_acc_n
        o0 = c.s_o_n

        @bk.sync
        def _(sy):
            sy.wait_ge(c.s_dv, dvv)
            sy.wait_ge(c.s_sc, scv)
            sy.wait_ge(c.s_acc, accv)
            sy.dma_start(t, sb[:, :]).then_inc(c.s_o, 16)
            sy.wait_ge(c.s_o, o0 + 16)
        bump(c, "s_o", 16)
    c.tap_list.append(name)



def _host_inputs(L0_attn_x, L0_attn_Wq, L0_attn_Wk, L0_attn_Wv, L0_attn_memory,
                 L0_attn_memory_norm, L0_attn_memory_weight, L0_ffn_router_w,
                 L0_ffn_gate_w, L0_ffn_up_w, L0_ffn_down_w, L0_ffn_s_gate_w,
                 L0_ffn_s_gate_b, L0_ffn_s_up_w, L0_ffn_s_up_b, L0_ffn_s_down_w,
                 L0_ffn_s_down_b):
    f = np.float32
    shared = dict(
        wqT=np.ascontiguousarray(np.asarray(L0_attn_Wq, f).T),
        wkT=np.ascontiguousarray(np.asarray(L0_attn_Wk, f).T),
        wvT=np.ascontiguousarray(np.asarray(L0_attn_Wv, f).T),
        mem=np.ascontiguousarray(np.asarray(L0_attn_memory, f)),
        memnorm=np.ascontiguousarray(np.asarray(L0_attn_memory_norm, f).reshape(-1)),
        mweight=np.ascontiguousarray(np.asarray(L0_attn_memory_weight, f).reshape(1)),
        routerT=np.ascontiguousarray(np.asarray(L0_ffn_router_w, f).T),
        gate=np.ascontiguousarray(np.asarray(L0_ffn_gate_w, f)),
        up=np.ascontiguousarray(np.asarray(L0_ffn_up_w, f)),
        down=np.ascontiguousarray(np.asarray(L0_ffn_down_w, f)),
        sgate=np.ascontiguousarray(np.asarray(L0_ffn_s_gate_w, f)),
        sup=np.ascontiguousarray(np.asarray(L0_ffn_s_up_w, f)),
        sdown=np.ascontiguousarray(np.asarray(L0_ffn_s_down_w, f)),
        sgb=np.ascontiguousarray(np.asarray(L0_ffn_s_gate_b, f)),
        supb=np.ascontiguousarray(np.asarray(L0_ffn_s_up_b, f)),
        sdb=np.ascontiguousarray(np.asarray(L0_ffn_s_down_b, f)),
    )
    x = np.asarray(L0_attn_x, f)
    in_maps = []
    for b in range(8):
        m = dict(shared)
        m["xT"] = np.ascontiguousarray(x[b].T)
        in_maps.append(m)
    return in_maps


def kernel(**inputs):
    in_maps = _host_inputs(**inputs)
    nc, _ = build_program()
    res = run_bass_kernel_spmd(nc, in_maps, core_ids=list(range(8)))
    out = np.stack([np.asarray(r["outT"], np.float32).T for r in res.results], axis=0)
    return np.ascontiguousarray(out)


# version 21
# speedup vs baseline: 1.0580x; 1.0580x over previous
import numpy as np
import concourse.bass as bass
import concourse.mybir as mybir
from concourse.bass_utils import run_bass_kernel_spmd

F32 = mybir.dt.float32
BF16 = mybir.dt.bfloat16
AX = mybir.AxisListType.X
ALU = mybir.AluOpType
AF = mybir.ActivationFunctionType

P = 128
D = 2048
S = 2048
NCH = 16
BLK = 512
NBLK = 4
DKV = 512
NQG = 4
NH = 8
DH = 64
IFF = 1408
ICH = 11
NEXP = 10
EPS = 1e-5
SCALE = 0.125

GROUPS = [(0, 3), (3, 6), (6, 9), (9, 12), (12, 15), (15, 16)]


def _sz(dtype):
    return 2 if dtype == BF16 else 4


class Arena:
    def __init__(self, nc, base, limit):
        self.nc, self.off, self.limit = nc, base, limit

    def sb(self, name, shape, dtype):
        nbytes = int(np.prod(shape[1:])) * _sz(dtype)
        t = self.nc.alloc_sbuf_tensor_at(name, list(shape), dtype, offset=self.off)
        self.off += (nbytes + 63) & ~63
        assert self.off <= self.limit, f"SBUF overflow at {name}: {self.off}"
        return t


class Ctx:
    pass


def bump(c, nm, k=1):
    setattr(c, nm + "_n", getattr(c, nm + "_n") + k)
    return getattr(c, nm + "_n")


class EngProxy:

    def __init__(self, c, eng, sem_name):
        self._c = c
        self.raw = eng
        self._sem = getattr(c, sem_name)
        self._nm = sem_name

    def __getattr__(self, name):
        fn = getattr(self.raw, name)

        def call(*a, **k):
            c = self._c
            self.raw.wait_ge(self._sem, getattr(c, self._nm + "_n"))
            ins = fn(*a, **k)
            ins.then_inc(self._sem, 1)
            bump(c, self._nm)
            return ins

        return call


def build_program(taps=(), phases=("P", "A", "B", "C")):
    nc = bass.Bass("TRN2", target_bir_lowering=False, debug=False)
    c = Ctx()
    c.nc = nc
    c.taps = set(taps)

    def din(name, shape):
        return nc.dram_tensor(name, list(shape), F32, kind="ExternalInput").ap()

    c.xT = din("xT", [D, S])
    c.wqT = din("wqT", [D, D])
    c.wkT = din("wkT", [D, DKV])
    c.wvT = din("wvT", [D, DKV])
    c.mem = din("mem", [DKV, DKV])
    c.memnorm = din("memnorm", [DKV])
    c.mweight = din("mweight", [1])
    c.routerT = din("routerT", [D, 8])
    c.gate = din("gate", [8, D, IFF])
    c.up = din("up", [8, D, IFF])
    c.down = din("down", [8, IFF, D])
    c.sgate = din("sgate", [D, 2 * IFF])
    c.sup = din("sup", [D, 2 * IFF])
    c.sdown = din("sdown", [2 * IFF, D])
    c.sgb = din("sgb", [2 * IFF])
    c.supb = din("supb", [2 * IFF])
    c.sdb = din("sdb", [D])
    c.outT = nc.dram_tensor("outT", [D, S], F32, kind="ExternalOutput").ap()

    c.xln1T_scr = nc.dram_tensor("xln1T_scr", [P, NCH, S], BF16, kind="Internal").ap()
    c.wq_scr = nc.dram_tensor("wq_scr", [P, NCH, D], BF16, kind="Internal").ap()
    c.x2_scr = nc.dram_tensor("x2_scr", [P, NCH, S], F32, kind="Internal").ap()
    c.gu_scr = nc.dram_tensor("gu_scr", [P, 320, IFF], BF16, kind="Internal").ap()
    c.dn_scr = nc.dram_tensor("dn_scr", [P, 110, D], BF16, kind="Internal").ap()

    c.psA = nc.alloc_psum_tensor("psA", [P, 1536], F32)
    c.psB = nc.alloc_psum_tensor("psB", [P, 1536], F32)
    c.psC = nc.alloc_psum_tensor("psC", [P, 512], F32)
    c.psD = nc.alloc_psum_tensor("psD", [P, 512], F32)

    BASE0 = (nc._sbuf_addr_for_side("left") + 63) & ~63
    LIM = nc._sbuf_addr_for_side("right") - 64
    per = Arena(nc, BASE0, LIM)
    c.ones_row = per.sb("ones_row", [1, P], F32)
    c.ones_col_bf = per.sb("ones_col_bf", [P, 1], BF16)
    c.one_one = per.sb("one_one", [1, 1], F32)
    c.w_ap = per.sb("w_ap", [1, 1], F32)
    c.lnw_ap = per.sb("lnw_ap", [1, 1], F32)
    c.w1m_col = per.sb("w1m_col", [1, DH], F32)
    c.w1m_cT = per.sb("w1m_cT", [P, 1], F32)
    c.w_cT = per.sb("w_cT", [P, 1], F32)
    c.ident = per.sb("ident", [P, P], F32)
    c.iota = per.sb("iota", [P, P], F32)
    c.mem_sb = per.sb("mem_sb", [P, 4 * 512], BF16)
    c.mn_sb = per.sb("mn_sb", [P, 4], BF16)
    c.routerT_sb = per.sb("routerT_sb", [P, NCH * 8], BF16)
    c.gb_col = per.sb("gb_col", [P, 22], F32)
    c.ub_col = per.sb("ub_col", [P, 22], F32)
    c.db_col = per.sb("db_col", [P, NCH], F32)
    c.mwrow = per.sb("mwrow", [1, 1], F32)
    c.rows = per.sb("rows", [1, 8 * 512 + 16], F32)
    PBASE = (per.off + 63) & ~63

    ab = Arena(nc, PBASE, LIM)
    c.xblk = ab.sb("xblk", [P, NCH * 512], F32)
    c.xln1T = ab.sb("xln1T", [P, NCH * 512], BF16)
    c.kT = ab.sb("kT", [P, 4 * S], BF16)
    c.v_aug = ab.sb("v_aug", [P, NCH * 640 + 64], BF16)
    c.tbf0 = ab.sb("tbf0", [P, 512], BF16)
    c.tbf1 = ab.sb("tbf1", [P, 512], BF16)
    c.tsq0 = ab.sb("tsq0", [P, 512], BF16)
    c.tsq1 = ab.sb("tsq1", [P, 512], BF16)
    c.tmpA = ab.sb("tmpA", [P, 512], F32)
    AB_SHARED_END = ab.off
    a_ar = Arena(nc, AB_SHARED_END, LIM)
    c.wkT_sb = a_ar.sb("wkT_sb", [P, NCH * 512], BF16)
    c.wvT_sb = a_ar.sb("wvT_sb", [P, NCH * 512], BF16)
    b_ar = Arena(nc, AB_SHARED_END, LIM)
    c.qT = b_ar.sb("qT", [P, NCH * 512], BF16)
    pe0_off = b_ar.off
    c.probs_e0 = b_ar.sb("probs_e0", [P, NCH * 512], BF16)
    pe1_off = b_ar.off
    c.probs_e1 = b_ar.sb("probs_e1", [P, NCH * 512], BF16)
    po0_off = b_ar.off
    c.probs_o0 = b_ar.sb("probs_o0", [P, NCH * 512], BF16)
    c.probs_o1 = b_ar.sb("probs_o1", [P, NCH * 512], BF16)
    c.wq_kc0 = b_ar.sb("wq_kc0", [P, D], BF16)
    c.wq_kc1 = b_ar.sb("wq_kc1", [P, D], BF16)
    c.rzB_sb = b_ar.sb("rzB_sb", [DH, 512], F32)
    c.tmp2a = b_ar.sb("tmp2a", [DH, 512], F32)
    c.tmp2b = b_ar.sb("tmp2b", [DH, 512], F32)
    al1 = Arena(nc, pe0_off, LIM)
    c.sigmaT = al1.sb("sigmaT", [P, NCH * 512], BF16)
    al2 = Arena(nc, pe1_off, LIM)
    c.et1 = al2.sb("et1", [P, 1024], F32)
    c.et2 = al2.sb("et2", [P, 1024], F32)
    al3 = Arena(nc, po0_off, LIM)
    c.amB = al3.sb("amB", [P, 512], F32)
    AB_END = max(a_ar.off, b_ar.off)

    c_ar = Arena(nc, PBASE, LIM)
    c.acc = c_ar.sb("acc", [P, NCH * 512], F32)
    c.xln2T = c_ar.sb("xln2T", [P, NCH * 512], BF16)
    c.slot0 = c_ar.sb("slot0", [P, NCH * IFF], BF16)
    c.slot1 = c_ar.sb("slot1", [P, NCH * IFF], BF16)
    c.silu_g = c_ar.sb("silu_g", [P, ICH * 512], BF16)
    c.h_sb = c_ar.sb("h_sb", [P, ICH * 512], BF16)
    c.cbB = c_ar.sb("cbB", [P, 512], BF16)
    c.combine = c_ar.sb("combine", [P, 32], F32)
    c.cb_row = c_ar.sb("cb_row", [1, 512], F32)
    c.ctbf0 = c_ar.sb("ctbf0", [P, 512], BF16)
    c.ctbf1 = c_ar.sb("ctbf1", [P, 512], BF16)
    c.ctsq0 = c_ar.sb("ctsq0", [P, 512], BF16)
    c.ctsq1 = c_ar.sb("ctsq1", [P, 512], BF16)
    c.ctmp = c_ar.sb("ctmp", [P, 512], F32)
    c.sx0 = c_ar.sb("sx0", [P, 512], F32)
    c.sx1 = c_ar.sb("sx1", [P, 512], F32)
    c.xs2 = c_ar.sb("xs2", [P, 512], F32)
    c.xs3 = c_ar.sb("xs3", [P, 512], F32)
    c.xs_bufs = [c.sx0, c.sx1, c.xs2, c.xs3]
    c.sxb0 = c_ar.sb("sxb0", [P, 512], BF16)
    c.sxb1 = c_ar.sb("sxb1", [P, 512], BF16)
    c.sxq0 = c_ar.sb("sxq0", [P, 512], BF16)
    c.sxq1 = c_ar.sb("sxq1", [P, 512], BF16)
    c.aB_sb = c_ar.sb("aB_sb", [P, 512], F32)
    c.bB_sb = c_ar.sb("bB_sb", [P, 512], F32)

    c.exp32 = c_ar.sb("exp32", [P, 32], F32)
    c.r8a = c_ar.sb("r8a", [P, 8], F32)
    c.r8b = c_ar.sb("r8b", [P, 8], F32)
    c.r1a = c_ar.sb("r1a", [P, 4], F32)
    C_END = c_ar.off
    assert max(AB_END, C_END) <= LIM, (AB_END, C_END)

    pr = Arena(nc, AB_SHARED_END, LIM)
    c.row_big = pr.sb("row_big", [1, 2 * IFF], F32)
    st_ar = Arena(nc, po0_off, LIM)
    c.stage0 = st_ar.sb("stage0", [P, D], BF16)
    c.stage1 = st_ar.sb("stage1", [P, D], BF16)

    for nm in ("s_x", "s_wm", "s_wg", "s_wu", "s_wd", "s_wq0", "s_wq1",
               "s_sc", "s_acc", "s_o", "s_cv", "s_se0", "s_se1",
               "s_xs0", "s_xs1",
               "s_pe", "s_dv", "s_ac", "s_gp", "s_bar"):
        setattr(c, nm, nc.alloc_semaphore(nm))
        setattr(c, nm + "_n", 0)
    c.s_ob = [nc.alloc_semaphore("s_ob%d" % b) for b in range(4)]
    c.s_ob_n = [0, 0, 0, 0]
    c.u_pe_end = 0
    c.last_cb_dv = 0
    c.bcast_ac_done = 0
    c.tap_list = []
    c.conv_chunks = _conv_chunks(c)
    c.conv_cursor = 0

    with nc.Block() as bigblk:
        if "P" in phases:
            emit_prologue(c, bigblk)
            barrier(c, bigblk)
        if "A" in phases:
            for blk in range(NBLK):
                emit_phase_a(c, bigblk, blk)
            barrier(c, bigblk)
        if "B" in phases:
            for blk in range(NBLK):
                emit_phase_b(c, bigblk, blk)
            barrier(c, bigblk)
        if "C" in phases:
            for st in range(NBLK):
                emit_phase_c(c, bigblk, st)
        emit_epilogue(c, bigblk)
    return nc, c


def barrier(c, bigblk):
    tgt = c.s_bar_n + 5
    for name in ("tensor", "vector", "scalar", "gpsimd", "sync"):
        def body(eng, _t=tgt):
            eng.nop().then_inc(c.s_bar, 1)
            eng.wait_ge(c.s_bar, _t)
        getattr(bigblk, name)(body)
    c.s_bar_n = tgt


def _conv_chunks(c):
    out = []
    for e in range(NEXP):
        shared = e >= 8
        half = e - 8
        if not shared:
            g = c.gate[e].rearrange("(cc p) i -> p cc i", p=P)
            u = c.up[e].rearrange("(cc p) i -> p cc i", p=P)
            d = c.down[e].rearrange("(cc p) o -> p cc o", p=P)
        else:
            g = c.sgate.rearrange("(cc p) i -> p cc i", p=P)[:, :, half * IFF:(half + 1) * IFF]
            u = c.sup.rearrange("(cc p) i -> p cc i", p=P)[:, :, half * IFF:(half + 1) * IFF]
            d = c.sdown[half * IFF:(half + 1) * IFF, :].rearrange("(cc p) o -> p cc o", p=P)
        for kc in range(NCH):
            out.append((g[:, kc, :], c.gu_scr[:, 32 * e + kc, :], IFF))
        for kc in range(NCH):
            out.append((u[:, kc, :], c.gu_scr[:, 32 * e + 16 + kc, :], IFF))
        for ic in range(ICH):
            out.append((d[:, ic, :], c.dn_scr[:, 11 * e + ic, :], D))
    return out


def emit_convert_batch(c, bigblk, lo, hi, pe_gate=None):
    chunks = c.conv_chunks[lo:hi]
    if not chunks:
        return

    def gb(g):
        for (srcap, dstap, width) in chunks:
            g.dma_start(dstap, srcap).then_inc(c.s_cv, 16)
            bump(c, "s_cv", 16)
    bigblk.gpsimd(gb)



def emit_prologue(c, bigblk):
    nc = c.nc
    blk = bigblk
    if True:
        @blk.sync
        def _(sy):
            sy.dma_start(c.mwrow[0:1, :], c.mweight[None, :]).then_inc(c.s_x, 16)
            sy.dma_start(c.rows[0:1, 0:DKV], c.memnorm[None, :]).then_inc(c.s_x, 16)
            sy.dma_start(c.row_big[0:1, :], c.sgb[None, :]).then_inc(c.s_x, 16)
        bump(c, "s_x", 48)

        @blk.gpsimd
        def _(g):
            gp = EngProxy(c, g, "s_gp")
            gp.iota(c.iota[:, :], pattern=[[1, P]], base=0, channel_multiplier=-1,
                    allow_small_or_imprecise_dtypes=True)
            g.dma_start(c.mem_sb[:, :].rearrange("p (c o) -> p c o", c=4),
                        c.mem.rearrange("(c p) o -> p c o", p=P)).then_inc(c.s_wm, 16)
            g.dma_start(c.routerT_sb[:, :].rearrange("p (c e) -> p c e", c=NCH),
                        c.routerT.rearrange("(c p) e -> p c e", p=P)).then_inc(c.s_wm, 16)
            c_w = 32
            for kc in range(NCH):
                stage = (c.stage0, c.stage1)[kc % 2]
                if kc >= 2:
                    g.wait_ge(c.s_sc, 16 * kc)
                g.dma_start(stage[:, :], c.wqT.rearrange("(c p) o -> p c o", p=P)[:, kc, :]
                            ).then_inc(c.s_wm, 16)
                c_w += 16
                g.wait_ge(c.s_wm, c_w)
                g.dma_start(c.wq_scr[:, kc, :], stage[:, :]).then_inc(c.s_sc, 16)
        bump(c, "s_wm", 32 + 16 * NCH)
        bump(c, "s_sc", 16 * NCH)
        gp_iota = c.s_gp_n

        @blk.scalar
        def _(s):
            sp = EngProxy(c, s, "s_ac")
            s.wait_ge(c.s_x, 48)
            sp.activation(c.w_ap[0:1, :], c.mwrow[0:1, :], AF.Sigmoid)
            sp.activation(c.lnw_ap[0:1, :], c.w_ap[0:1, :], AF.Ln)

        @blk.vector
        def _(v):
            vp = EngProxy(c, v, "s_dv")
            vp.memset(c.ones_row[:, :], 1.0)
            vp.memset(c.ones_col_bf[:, :], 1.0)
            vp.memset(c.one_one[:, :], 1.0)
            vp.memset(c.v_aug[:, :], 1.0)
            v.wait_ge(c.s_gp, gp_iota)
            vp.tensor_single_scalar(c.ident[:, :], c.iota[:, :], 0.0, ALU.is_equal)
            v.wait_ge(c.s_ac, 1)
            vp.tensor_scalar(c.w1m_col[0:1, :], c.ones_row[0:1, 0:DH], c.w_ap[0:1, 0:1],
                             None, ALU.subtract)
            v.wait_ge(c.s_pe, 1)
            vp.tensor_copy(c.mn_sb[:, 0:4], c.psC[:, 0:4])
            v.wait_ge(c.s_pe, 2)
            vp.tensor_copy(c.gb_col[:, :], c.psC[:, 0:22])
            v.wait_ge(c.s_pe, 3)
            vp.tensor_copy(c.ub_col[:, :], c.psC[:, 0:22])
            v.wait_ge(c.s_pe, 4)
            vp.tensor_copy(c.db_col[:, :], c.psC[:, 0:16])
            vp.tensor_scalar(c.rows[0:1, 4096:4097], c.one_one[0:1, 0:1],
                             c.w_ap[0:1, 0:1], None, ALU.subtract)
        @blk.tensor
        def _(t):
            t.wait_ge(c.s_x, 48)
            t.wait_ge(c.s_dv, 4)
            for j in range(4):
                ins = t.matmul(c.psC[:, j:j + 1], c.rows[0:1, j * P:(j + 1) * P],
                               c.one_one[0:1, 0:1], start=True, stop=True)
            ins.then_inc(c.s_pe, 1)
            t.wait_ge(c.s_dv, 7)
            for j in range(22):
                ins = t.matmul(c.psC[:, j:j + 1], c.row_big[0:1, j * P:(j + 1) * P],
                               c.one_one[0:1, 0:1], start=True, stop=True)
            ins.then_inc(c.s_pe, 1)
        bump(c, "s_pe", 2)

    blk2 = bigblk
    if True:
        @blk2.sync
        def _(sy):
            sy.wait_ge(c.s_pe, 2)
            sy.dma_start(c.row_big[0:1, :], c.supb[None, :]).then_inc(c.s_x, 16)
            sy.wait_ge(c.s_pe, 3)
            sy.dma_start(c.row_big[0:1, 0:D], c.sdb[None, :]).then_inc(c.s_x, 16)
        bump(c, "s_x", 32)

        @blk2.tensor
        def _(t):
            t.wait_ge(c.s_x, 64)
            t.wait_ge(c.s_dv, 8)
            for j in range(22):
                ins = t.matmul(c.psC[:, j:j + 1], c.row_big[0:1, j * P:(j + 1) * P],
                               c.one_one[0:1, 0:1], start=True, stop=True)
            ins.then_inc(c.s_pe, 1)
            t.wait_ge(c.s_x, 80)
            t.wait_ge(c.s_dv, 9)
            for j in range(16):
                ins = t.matmul(c.psC[:, j:j + 1], c.row_big[0:1, j * P:(j + 1) * P],
                               c.one_one[0:1, 0:1], start=True, stop=True)
            ins.then_inc(c.s_pe, 1)
        bump(c, "s_pe", 2)

        dv_w1 = c.s_dv_n
        pe_w0 = c.s_pe_n

        @blk2.tensor
        def _(t):
            t.wait_ge(c.s_dv, dv_w1)
            t.matmul(c.psC[:, 0:1], c.ones_row[0:1, :], c.rows[0:1, 4096:4097],
                     start=True, stop=True).then_inc(c.s_pe, 1)
            t.matmul(c.psC[:, 1:2], c.ones_row[0:1, :], c.w_ap[0:1, 0:1],
                     start=True, stop=True).then_inc(c.s_pe, 1)
        bump(c, "s_pe", 2)

        @blk2.vector
        def _(v):
            vp = EngProxy(c, v, "s_dv")
            v.wait_ge(c.s_pe, pe_w0 + 2)
            vp.tensor_copy(c.w1m_cT[:, :], c.psC[:, 0:1])
            vp.tensor_copy(c.w_cT[:, :], c.psC[:, 1:2])
        c.prologue_pe = c.s_pe_n
        c.wq_stage_sc = c.s_sc_n



def make_ln_bodies(c, src, dst, bufs, pre_wait_v=None, pre_wait_s=None,
                   dst_war_wait=None, chunk_wait_v=None, chunk_wait_s=None):
    tbf = (bufs[0], bufs[1])
    tsq = (bufs[2], bufs[3])
    tmp = bufs[4]
    rows = c.rows
    mrow = rows[0:1, 0:512]
    vrow = rows[0:1, 512:1024]
    msq = rows[0:1, 1024:1536]
    arow = rows[0:1, 1536:2048]
    brow = rows[0:1, 2048:2560]

    dv0 = c.s_dv_n
    ac0 = c.s_ac_n
    pe0 = c.s_pe_n

    def vbody(v):
        vp = EngProxy(c, v, "s_dv")
        if pre_wait_v:
            pre_wait_v(v)
        for ch in range(NCH):
            if chunk_wait_v:
                chunk_wait_v(v, ch)
            if ch >= 2:
                v.wait_ge(c.s_pe, pe0 + 2 * (ch - 1))
            vp.tensor_copy(tbf[ch % 2][:, :], src[:, 512 * ch:512 * (ch + 1)])
        v.wait_ge(c.s_pe, pe0 + 32)
        vp.tensor_scalar(mrow, c.psC[0:1, :], 1.0 / D, None, ALU.mult)
        vp.tensor_tensor(msq, mrow, mrow, ALU.mult)
        vp.tensor_scalar(vrow, c.psD[0:1, :], 1.0 / D, EPS, ALU.mult, ALU.add)
        vp.tensor_tensor(vrow, vrow, msq, ALU.subtract)
        v.wait_ge(c.s_ac, ac0 + 18)
        vp.scalar_tensor_tensor(brow, mrow, -1.0, arow, ALU.mult, ALU.mult)
        v.wait_ge(c.s_pe, pe0 + 34)
        if dst_war_wait:
            dst_war_wait(v)
        for ch in range(NCH):
            vp.tensor_tensor(tmp[:, :], src[:, 512 * ch:512 * (ch + 1)],
                             c.psC[:, :], ALU.mult)
            vp.tensor_tensor(dst[:, 512 * ch:512 * (ch + 1)], tmp[:, :],
                             c.psD[:, :], ALU.add)

    def sbody(s):
        sp = EngProxy(c, s, "s_ac")
        if pre_wait_s:
            pre_wait_s(s)
        for ch in range(NCH):
            if chunk_wait_s:
                chunk_wait_s(s, ch)
            if ch >= 2:
                s.wait_ge(c.s_pe, pe0 + 2 * ch)
            sp.activation(tsq[ch % 2][:, :], src[:, 512 * ch:512 * (ch + 1)], AF.Square)
        s.wait_ge(c.s_dv, dv0 + 20)
        sp.activation(arow, vrow, AF.Ln)
        sp.activation(arow, arow, AF.Exp, scale=-0.5)

    def tbody(t):
        for ch in range(NCH):
            t.wait_ge(c.s_dv, dv0 + ch + 1)
            t.matmul(c.psC[0:1, :], c.ones_col_bf[:, 0:1], tbf[ch % 2][:, :],
                     start=(ch == 0), stop=(ch == NCH - 1)).then_inc(c.s_pe, 1)
            bump(c, "s_pe")
            t.wait_ge(c.s_ac, ac0 + ch + 1)
            t.matmul(c.psD[0:1, :], c.ones_col_bf[:, 0:1], tsq[ch % 2][:, :],
                     start=(ch == 0), stop=(ch == NCH - 1)).then_inc(c.s_pe, 1)
            bump(c, "s_pe")
        t.wait_ge(c.s_dv, dv0 + 21)
        t.matmul(c.psC[:, :], c.ones_row[0:1, :], arow, start=True, stop=True
                 ).then_inc(c.s_pe, 1)
        bump(c, "s_pe")
        t.matmul(c.psD[:, :], c.ones_row[0:1, :], brow, start=True, stop=True
                 ).then_inc(c.s_pe, 1)
        bump(c, "s_pe")

    return vbody, sbody, tbody


def emit_c_stats_early(c, bigblk, st):
    scol = st * BLK
    se0_0 = c.s_se0_n
    se1_0 = c.s_se1_n
    dv0 = c.s_dv_n
    ac0 = c.s_ac_n
    pe0 = c.s_pe_n

    def sy(s):
        for ch in range(NCH):
            par = ch % 2
            if ch >= 2:
                s.wait_ge(c.s_dv, dv0 + ch - 1)
                s.wait_ge(c.s_ac, ac0 + ch - 1)
            s.dma_start((c.sx0, c.sx1)[par][:, :],
                        c.x2_scr[:, ch, scol:scol + BLK]
                        ).then_inc((c.s_se0, c.s_se1)[par], 16)
            bump(c, "s_se0" if par == 0 else "s_se1", 16)
    bigblk.sync(sy)

    def vb(v):
        vp = EngProxy(c, v, "s_dv")
        for ch in range(NCH):
            par = ch % 2
            v.wait_ge((c.s_se0, c.s_se1)[par],
                      (se0_0, se1_0)[par] + 16 * (ch // 2 + 1))
            if ch >= 2:
                v.wait_ge(c.s_pe, pe0 + 2 * (ch - 2) + 1)
            vp.tensor_copy((c.sxb0, c.sxb1)[par][:, :], (c.sx0, c.sx1)[par][:, :])
    bigblk.vector(vb)

    def sb(s):
        sp = EngProxy(c, s, "s_ac")
        for ch in range(NCH):
            par = ch % 2
            s.wait_ge((c.s_se0, c.s_se1)[par],
                      (se0_0, se1_0)[par] + 16 * (ch // 2 + 1))
            if ch >= 2:
                s.wait_ge(c.s_pe, pe0 + 2 * (ch - 2) + 2)
            sp.activation((c.sxq0, c.sxq1)[par][:, :], (c.sx0, c.sx1)[par][:, :],
                          AF.Square)
    bigblk.scalar(sb)

    def tb(t):
        for ch in range(NCH):
            par = ch % 2
            t.wait_ge(c.s_dv, dv0 + ch + 1)
            t.matmul(c.psC[0:1, :], c.ones_col_bf[:, 0:1], (c.sxb0, c.sxb1)[par][:, :],
                     start=(ch == 0), stop=(ch == NCH - 1),
                     skip_group_check=True).then_inc(c.s_pe, 1)
            bump(c, "s_pe")
            t.wait_ge(c.s_ac, ac0 + ch + 1)
            t.matmul(c.psC[32:33, :], c.ones_col_bf[:, 0:1], (c.sxq0, c.sxq1)[par][:, :],
                     start=(ch == 0), stop=(ch == NCH - 1),
                     skip_group_check=True).then_inc(c.s_pe, 1)
            bump(c, "s_pe")
    bigblk.tensor(tb)
    c.c_stats_pe_end = c.s_pe_n


def emit_ln_rows_early(c, bigblk):
    rows = c.rows
    mrow = rows[0:1, 0:512]
    vrow = rows[0:1, 512:1024]
    msq = rows[0:1, 1024:1536]
    arow = rows[0:1, 1536:2048]
    brow = rows[0:1, 2048:2560]
    stats_pe = c.c_stats_pe_end
    dv0 = c.s_dv_n
    ac0 = c.s_ac_n

    def vbody(v):
        vp = EngProxy(c, v, "s_dv")
        v.wait_ge(c.s_pe, stats_pe)
        vp.tensor_scalar(mrow, c.psC[0:1, :], 1.0 / D, None, ALU.mult)
        vp.tensor_tensor(msq, mrow, mrow, ALU.mult)
        vp.tensor_scalar(vrow, c.psC[32:33, :], 1.0 / D, EPS, ALU.mult, ALU.add)
        vp.tensor_tensor(vrow, vrow, msq, ALU.subtract)
        v.wait_ge(c.s_ac, ac0 + 2)
        vp.scalar_tensor_tensor(brow, mrow, -1.0, arow, ALU.mult, ALU.mult)
    bigblk.vector(vbody)

    def sbody(s):
        sp = EngProxy(c, s, "s_ac")
        s.wait_ge(c.s_dv, dv0 + 4)
        sp.activation(arow, vrow, AF.Ln)
        sp.activation(arow, arow, AF.Exp, scale=-0.5)
    bigblk.scalar(sbody)
    c.rows_dv_done = dv0 + 5


def emit_ln_bcast_early(c, bigblk):
    arow = c.rows[0:1, 1536:2048]
    brow = c.rows[0:1, 2048:2560]
    dv_rows = c.rows_dv_done
    cb_dv = c.last_cb_dv
    pe_a = c.s_pe_n + 1
    pe_b = c.s_pe_n + 2
    ac_a = c.s_ac_n + 1

    def tbody(t):
        t.wait_ge(c.s_dv, max(dv_rows, cb_dv))
        t.matmul(c.psD[:, :], c.ones_row[0:1, :], arow, start=True, stop=True
                 ).then_inc(c.s_pe, 1)
        t.wait_ge(c.s_ac, ac_a)
        t.matmul(c.psD[:, :], c.ones_row[0:1, :], brow, start=True, stop=True
                 ).then_inc(c.s_pe, 1)
    bigblk.tensor(tbody)
    bump(c, "s_pe", 2)

    def sbody(s):
        sp = EngProxy(c, s, "s_ac")
        s.wait_ge(c.s_pe, pe_a)
        sp.activation(c.aB_sb[:, :], c.psD[:, :], AF.Copy)
        s.wait_ge(c.s_pe, pe_b)
        sp.activation(c.bB_sb[:, :], c.psD[:, :], AF.Copy)
    bigblk.scalar(sbody)
    c.bcast_ac_done = c.s_ac_n




def emit_phase_a(c, bigblk, blki):
    nc = c.nc
    scol = blki * BLK

    bk = bigblk
    if True:
        x_base = c.s_x_n
        if blki == 0:
            @bk.sync
            def _(sy):
                for j in range(4):
                    sy.dma_start(
                        c.xblk[:, :].rearrange("p (c s) -> p c s", c=NCH)[:, 4 * j:4 * j + 4, :],
                        c.xT.rearrange("(c p) s -> p c s", p=P)[:, 4 * j:4 * j + 4, scol:scol + BLK]
                    ).then_inc(c.s_x, 16)
            bump(c, "s_x", 64)
        else:
            dv0p = c.a_ln_dv0

            @bk.sync
            def _(sy):
                for ch in range(NCH):
                    sy.wait_ge(c.s_dv, dv0p + 23 + 2 * ch)
                    if ch >= 1:
                        sy.wait_ge(c.s_x, x_base + 16 * ch)
                    sy.dma_start(
                        c.xblk[:, :].rearrange("p (c s) -> p c s", c=NCH)[:, ch:ch + 1, :],
                        c.xT.rearrange("(c p) s -> p c s", p=P)[:, ch:ch + 1, scol:scol + BLK]
                    ).then_inc(c.s_x, 16)
            bump(c, "s_x", 256)
        x_ready = c.s_x_n

        if blki == 0:
            pro_pe = c.prologue_pe

            @bk.gpsimd
            def _(g):
                g.wait_ge(c.s_pe, pro_pe)
                for j in range(4):
                    g.dma_start(
                        c.wkT_sb[:, :].rearrange("p (c o) -> p c o", c=NCH)[:, 4 * j:4 * j + 4, :],
                        c.wkT.rearrange("(c p) o -> p c o", p=P)[:, 4 * j:4 * j + 4, :]
                    ).then_inc(c.s_wm, 16)
                for j in range(4):
                    g.dma_start(
                        c.wvT_sb[:, :].rearrange("p (c o) -> p c o", c=NCH)[:, 4 * j:4 * j + 4, :],
                        c.wvT.rearrange("(c p) o -> p c o", p=P)[:, 4 * j:4 * j + 4, :]
                    ).then_inc(c.s_wm, 16)
            bump(c, "s_wm", 128)
        wv_ready = c.s_wm_n
        sc_prev = c.s_sc_n

        c.a_ln_dv0 = c.s_dv_n
        if blki == 0:
            cwv = lambda v, ch: v.wait_ge(c.s_x, x_ready)
            cws = lambda s, ch: s.wait_ge(c.s_x, x_ready)
        else:
            cwv = lambda v, ch: v.wait_ge(c.s_x, x_base + 16 * (ch + 1))
            cws = lambda s, ch: s.wait_ge(c.s_x, x_base + 16 * (ch + 1))
        vb, sb, tb = make_ln_bodies(
            c, c.xblk, c.xln1T, (c.tbf0, c.tbf1, c.tsq0, c.tsq1, c.tmpA),
            chunk_wait_v=cwv,
            chunk_wait_s=cws,
            dst_war_wait=(None if blki == 0 else
                          (lambda v: v.wait_ge(c.s_sc, sc_prev))),
        )
        bk.vector(vb)
        bk.scalar(sb)
        bk.tensor(tb)
    xln_done = c.s_dv_n

    bk2 = bigblk
    if True:
        pe0 = c.s_pe_n

        @bk2.tensor
        def _(t):
            t.wait_ge(c.s_dv, xln_done)
            t.wait_ge(c.s_wm, wv_ready)
            for dkc in range(4):
                ps = (c.psA[:, 0:512], c.psA[:, 512:1024])[dkc % 2]
                if dkc >= 2:
                    t.wait_ge(c.s_dv, xln_done + dkc - 1)
                for kc in range(NCH):
                    ins = t.matmul(ps, c.wkT_sb[:, 512 * kc + 128 * dkc:512 * kc + 128 * (dkc + 1)],
                                   c.xln1T[:, 512 * kc:512 * (kc + 1)],
                                   start=(kc == 0), stop=(kc == NCH - 1))
                ins.then_inc(c.s_pe, 1)
                bump(c, "s_pe")
            for tc4 in range(4):
                ps = (c.psA[:, 1024:1536], c.psB[:, 0:512])[tc4 % 2]
                if tc4 >= 2:
                    t.wait_ge(c.s_dv, xln_done + 4 + tc4 - 1)
                for kc in range(NCH):
                    ins = t.matmul(ps, c.xln1T[:, 512 * kc + 128 * tc4:512 * kc + 128 * (tc4 + 1)],
                                   c.wvT_sb[:, 512 * kc:512 * (kc + 1)],
                                   start=(kc == 0), stop=(kc == NCH - 1))
                ins.then_inc(c.s_pe, 1)
                bump(c, "s_pe")

        @bk2.vector
        def _(v):
            vp = EngProxy(c, v, "s_dv")
            for dkc in range(4):
                ps = (c.psA[:, 0:512], c.psA[:, 512:1024])[dkc % 2]
                v.wait_ge(c.s_pe, pe0 + dkc + 1)
                vp.tensor_copy(c.kT[:, S * dkc + scol:S * dkc + scol + BLK], ps)
            for tc4 in range(4):
                ps = (c.psA[:, 1024:1536], c.psB[:, 0:512])[tc4 % 2]
                tc = 4 * blki + tc4
                v.wait_ge(c.s_pe, pe0 + 4 + tc4 + 1)
                vp.tensor_copy(
                    c.v_aug[:, 640 * tc:640 * (tc + 1)].rearrange("p (h e) -> p h e", h=NH)[:, :, 0:DH],
                    ps.rearrange("p (h e) -> p h e", e=DH)
                )

        @bk2.gpsimd
        def _(g):
            g.wait_ge(c.s_dv, xln_done)
            if blki == 0:
                g.wait_ge(c.s_sc, c.wq_stage_sc)
            for j in range(4):
                g.dma_start(
                    c.xln1T_scr[:, 4 * j:4 * j + 4, scol:scol + BLK],
                    c.xln1T[:, :].rearrange("p (c s) -> p c s", c=NCH)[:, 4 * j:4 * j + 4, :]
                ).then_inc(c.s_sc, 16)
        bump(c, "s_sc", 64)
    nconv = 16
    emit_convert_batch(c, bigblk, c.conv_cursor, c.conv_cursor + nconv)
    c.conv_cursor += nconv



def emit_phase_b(c, bigblk, blki):
    nc = c.nc
    scol = blki * BLK

    bk = bigblk
    if True:
        pe_prev = c.s_pe_n
        sc_prev = c.s_sc_n
        dv_prev = c.s_dv_n
        acc_prev = c.s_acc_n

        if blki == 0:
            @bk.sync
            def _(sy):
                sy.wait_ge(c.s_pe, pe_prev)
                sy.wait_ge(c.s_sc, sc_prev)
                sy.wait_ge(c.s_dv, dv_prev)
                sy.wait_ge(c.s_acc, acc_prev)
                for j in range(4):
                    sy.dma_start(
                        c.xblk[:, :].rearrange("p (c s) -> p c s", c=NCH)[:, 4 * j:4 * j + 4, :],
                        c.xT.rearrange("(c p) s -> p c s", p=P)[:, 4 * j:4 * j + 4, scol:scol + BLK]
                    ).then_inc(c.s_x, 16)
                for j in range(4):
                    sy.dma_start(
                        c.xln1T[:, :].rearrange("p (c s) -> p c s", c=NCH)[:, 4 * j:4 * j + 4, :],
                        c.xln1T_scr[:, 4 * j:4 * j + 4, scol:scol + BLK]
                    ).then_inc(c.s_x, 16)
            bump(c, "s_x", 128)
            xln1T_ready = ("x", c.s_x_n)
        else:
            q_pe_prev = c.q_pe_end
            ob_base = c.s_ob_n[0]

            @bk.sync
            def _(sy):
                sy.wait_ge(c.s_pe, q_pe_prev)
                sy.wait_ge(c.s_ob[0], ob_base)
                for j in range(4):
                    sy.dma_start(
                        c.xln1T[:, :].rearrange("p (c s) -> p c s", c=NCH)[:, 4 * j:4 * j + 4, :],
                        c.xln1T_scr[:, 4 * j:4 * j + 4, scol:scol + BLK]
                    ).then_inc(c.s_ob[0], 16)
            c.s_ob_n[0] += 64
            xln1T_ready = ("ob", c.s_ob_n[0])
            sc_base = c.prev_sc_store_base

            @bk.sync
            def _(sy):
                for ch in range(NCH):
                    sy.wait_ge(c.s_sc, sc_base + 16 * (ch + 1))
                    sy.dma_start(
                        c.xblk[:, :].rearrange("p (c s) -> p c s", c=NCH)[:, ch:ch + 1, :],
                        c.xT.rearrange("(c p) s -> p c s", p=P)[:, ch:ch + 1, scol:scol + BLK]
                    ).then_inc(c.s_x, 16)
            bump(c, "s_x", 256)
        x_full = c.s_x_n
        c.b_x_full = x_full

        wq0_0 = c.s_wq0_n
        wq1_0 = c.s_wq1_n
        pe0 = c.s_pe_n
        dv0 = c.s_dv_n

        @bk.sync
        def _(sy2):
            if blki == 0:
                sy2.wait_ge(c.s_sc, c.wq_stage_sc)
            for half in range(2):
                for kc in range(NCH):
                    idx = half * NCH + kc
                    buf = (c.wq_kc0, c.wq_kc1)[idx % 2]
                    sem = (c.s_wq0, c.s_wq1)[idx % 2]
                    if idx >= 2:
                        sy2.wait_ge(c.s_pe, pe0 + idx - 1)
                    sy2.dma_start(buf[:, :], c.wq_scr[:, kc, :]).then_inc(sem, 16)
        bump(c, "s_wq0", 256)
        bump(c, "s_wq1", 256)

        @bk.tensor
        def _(t):
            slices = [c.psA[:, 0:512], c.psA[:, 512:1024], c.psA[:, 1024:1536],
                      c.psB[:, 0:512], c.psB[:, 512:1024], c.psB[:, 1024:1536],
                      c.psC[:, :], c.psD[:, :]]
            if xln1T_ready[0] == "x":
                t.wait_ge(c.s_x, xln1T_ready[1])
            else:
                t.wait_ge(c.s_ob[0], xln1T_ready[1])
            t.wait_ge(c.s_dv, dv0)
            t.wait_ge(c.s_acc, acc_prev)
            for half in range(2):
                if half == 1:
                    t.wait_ge(c.s_dv, dv0 + 8)
                for kc in range(NCH):
                    idx = half * NCH + kc
                    buf = (c.wq_kc0, c.wq_kc1)[idx % 2]
                    t.wait_ge((c.s_wq0, c.s_wq1)[idx % 2],
                              (wq0_0, wq1_0)[idx % 2] + 16 * (idx // 2 + 1))
                    for j in range(8):
                        oc = half * 8 + j
                        ins = t.matmul(slices[j], buf[:, 128 * oc:128 * (oc + 1)],
                                       c.xln1T[:, 512 * kc:512 * (kc + 1)],
                                       start=(kc == 0), stop=(kc == NCH - 1),
                                       skip_group_check=True)
                    ins.then_inc(c.s_pe, 1)
                    bump(c, "s_pe")
        c.q_pe_end = c.s_pe_n

        @bk.vector
        def _(v):
            vp = EngProxy(c, v, "s_dv")
            slices = [c.psA[:, 0:512], c.psA[:, 512:1024], c.psA[:, 1024:1536],
                      c.psB[:, 0:512], c.psB[:, 512:1024], c.psB[:, 1024:1536],
                      c.psC[:, :], c.psD[:, :]]
            for half in range(2):
                v.wait_ge(c.s_pe, pe0 + NCH * (half + 1))
                for j in range(8):
                    oc = half * 8 + j
                    vp.tensor_copy(c.qT[:, 512 * oc:512 * (oc + 1)], slices[j])

    bk2 = bigblk
    if True:
        dv0 = c.s_dv_n
        ac0 = c.s_ac_n

        @bk2.vector
        def _(v):
            vp = EngProxy(c, v, "s_dv")
            if blki == 0:
                v.wait_ge(c.s_sc, c.wq_stage_sc)
            for gpi in range(8):
                sl = slice(1024 * gpi, 1024 * (gpi + 1))
                if gpi < 4:
                    v.wait_ge(c.s_dv, dv0 - 16 + 2 * gpi + 2)
                vp.tensor_scalar_min(c.et1[:, :], c.qT[:, sl], 0.0)
                vp.tensor_scalar_max(c.et2[:, :], c.qT[:, sl], 0.0)
                v.wait_ge(c.s_ac, ac0 + gpi + 1)
                vp.scalar_tensor_tensor(c.sigmaT[:, sl], c.et1[:, :], -1.0, c.et2[:, :],
                                        ALU.add, ALU.add)

        @bk2.scalar
        def _(s):
            sp = EngProxy(c, s, "s_ac")
            if blki == 0:
                s.wait_ge(c.s_sc, c.wq_stage_sc)
            for gpi in range(8):
                s.wait_ge(c.s_dv, dv0 + 3 * gpi + 1)
                sp.activation(c.et1[:, :], c.et1[:, :], AF.Exp)

    for qg in range(NQG):
        _emit_amem(c, bigblk, qg)

    c.cur_scol = scol
    c.prev_sc_store_base = c.s_sc_n
    _emit_pairs_pipelined(c, bigblk)

    bst = bigblk
    if True:
        dvv = c.s_dv_n
        accv = c.s_acc_n

        @bst.gpsimd
        def _(g):
            g.wait_ge(c.s_dv, dvv)
            g.wait_ge(c.s_acc, accv)
            g.wait_ge(c.s_sc, c.s_sc_n)
            g.dma_start(c.x2_scr[:, 15, scol:scol + BLK],
                        c.xblk[:, 512 * 15:512 * 16]).then_inc(c.s_sc, 16)
        bump(c, "s_sc", 16)
    if blki == 3:
        nconv = len(c.conv_chunks) - c.conv_cursor
        emit_convert_batch(c, bigblk, c.conv_cursor, c.conv_cursor + nconv)
        c.conv_cursor += nconv
    if "x2" in c.taps:
        _tap(c, bigblk, f"x2_{blki}", c.xblk, [P, NCH * 512], F32)
    if "qT" in c.taps and blki == 0:
        _tap(c, bigblk, f"qT_{blki}", c.qT, [P, NCH * 512], BF16)
    if "sigmaT" in c.taps and blki == 0:
        _tap(c, bigblk, f"sigmaT_{blki}", c.sigmaT, [P, NCH * 512], BF16)
    if "kT" in c.taps and blki == 0:
        _tap(c, bigblk, f"kT_{blki}", c.kT, [P, 4 * S], BF16)
    if "vaug" in c.taps and blki == 0:
        _tap(c, bigblk, f"vaug_{blki}", c.v_aug, [P, NCH * 520], BF16)


def _emit_amem(c, bigblk, qg):
    nc = c.nc
    rd = c.rows[0:1, 2560:3072]
    bk = bigblk
    if True:
        dv0 = c.s_dv_n
        pe0 = c.s_pe_n

        @bk.tensor
        def _(t):
            t.wait_ge(c.s_dv, dv0)
            for dkoc in range(4):
                ps = (c.psA[:, 0:512], c.psA[:, 512:1024], c.psA[:, 1024:1536],
                      c.psB[:, 0:512])[dkoc]
                for ci in range(4):
                    ins = t.matmul(ps, c.mem_sb[:, 512 * ci + 128 * dkoc:512 * ci + 128 * (dkoc + 1)],
                                   c.sigmaT[:, 512 * (4 * qg + ci):512 * (4 * qg + ci + 1)],
                                   start=(ci == 0), stop=(ci == 3))
            ins.then_inc(c.s_pe, 1)
            bump(c, "s_pe")
            for ci in range(4):
                ins = t.matmul(c.psB[0:1, 512:1024], c.mn_sb[:, ci:ci + 1],
                               c.sigmaT[:, 512 * (4 * qg + ci):512 * (4 * qg + ci + 1)],
                               start=(ci == 0), stop=(ci == 3))
            ins.then_inc(c.s_pe, 1)
            bump(c, "s_pe")
            t.wait_ge(c.s_dv, dv0 + 2)
            t.matmul(c.psB[:, 1024:1536], c.ones_row[0:1, :], rd, start=True, stop=True
                     ).then_inc(c.s_pe, 1)
            bump(c, "s_pe")

        @bk.vector
        def _(v):
            vp = EngProxy(c, v, "s_dv")
            v.wait_ge(c.s_x, c.b_x_full)
            v.wait_ge(c.s_pe, pe0 + 2)
            vp.reciprocal(rd, c.psB[0:1, 512:1024])
            vp.tensor_scalar(rd, rd, c.w_ap[0:1, 0:1], None, ALU.mult)
            v.wait_ge(c.s_pe, pe0 + 3)
            vp.tensor_copy(c.amB[:, :], c.psB[:, 1024:1536])
            for dkoc in range(4):
                ps = (c.psA[:, 0:512], c.psA[:, 512:1024], c.psA[:, 1024:1536],
                      c.psB[:, 0:512])[dkoc]
                ch = 4 * qg + dkoc
                vp.tensor_tensor(c.tmpA[:, :], ps, c.amB[:, :], ALU.mult)
                vp.tensor_tensor(c.xblk[:, 512 * ch:512 * (ch + 1)],
                                 c.xblk[:, 512 * ch:512 * (ch + 1)], c.tmpA[:, :],
                                 ALU.add)


def _emit_pairs_pipelined(c, bigblk):
    ac_base = c.s_ac_n
    dv_amem = c.s_dv_n
    states = []
    for n in range(17):
        cur = None
        if n < 16:
            qg, j = n // 4, n % 4
            cur = dict(n=n, oc=n, h_e=2 * j, h_o=2 * j + 1,
                       pbuf_e=(c.probs_e0, c.probs_e1)[n % 2],
                       pbuf_o=(c.probs_o0, c.probs_o1)[n % 2],
                       sc_e={}, sc_o={}, exp_e={}, exp_o={})
            base = ac_base + 16 * n
            for tp in range(8):
                cur["exp_e"][tp] = base + 2 * tp + 1
                cur["exp_o"][tp] = base + 2 * tp + 2
        prev = states[n - 1] if n >= 1 else None
        prev2 = states[n - 2] if n >= 2 else None
        _emit_slot(c, bigblk, cur, prev, prev2, dv_amem)
        states.append(cur)
        nconv = min(6, len(c.conv_chunks) - c.conv_cursor)
        if nconv:
            emit_convert_batch(c, bigblk, c.conv_cursor, c.conv_cursor + nconv)
            c.conv_cursor += nconv
    assert c.s_ac_n == ac_base + 256, c.s_ac_n - ac_base


def _pv_batch(c, t, prev, prev2, parity, lo, hi):
    pbuf = prev["pbuf_" + parity]
    h = prev["h_" + parity]
    bank = c.psC if parity == "e" else c.psD
    if lo == 0:
        t.wait_ge(c.s_ac, prev["exp_" + parity][7])
        if prev2 is not None:
            t.wait_ge(c.s_dv, prev2["tmp2_" + parity])
    for tc in range(lo, hi):
        ins = t.matmul(bank[:, :],
                       c.v_aug[:, 640 * tc + 80 * h:640 * tc + 80 * h + 128],
                       pbuf[:, 512 * tc:512 * (tc + 1)],
                       start=(tc == 0), stop=(tc == 15), skip_group_check=True)
    ins.then_inc(c.s_pe, 1)
    bump(c, "s_pe")
    if hi == 16:
        prev["pv_" + parity + "_done"] = c.s_pe_n


def _emit_slot(c, bigblk, cur, prev, prev2, dv_amem):
    nc = c.nc
    dv_base = c.s_dv_n
    if prev is not None:
        prev["zcopy_e"] = dv_base + 1
        prev["tmp2_e"] = dv_base + 3
        prev["x2e"] = dv_base + 4
        prev["zcopy_o"] = dv_base + 5
        prev["tmp2_o"] = dv_base + 7
    rz_e = c.rows[0:1, 3072:3584]
    rz_o = c.rows[0:1, 3584:4096]

    def tb(t):
        for tp in range(8):
            if prev is not None:
                if tp < 4:
                    _pv_batch(c, t, prev, prev2, "e", 4 * tp, 4 * tp + 4)
                else:
                    _pv_batch(c, t, prev, prev2, "o", 4 * (tp - 4), 4 * (tp - 4) + 4)
            if prev is not None and tp == 4:
                t.wait_ge(c.s_dv, prev["zcopy_e"])
                t.matmul(c.psA[0:DH, 1024:1536], c.ones_row[0:1, 0:DH], rz_e,
                         start=True, stop=True).then_inc(c.s_pe, 1)
                prev["bcast_e"] = bump(c, "s_pe")
            if cur is not None:
                if tp == 0:
                    t.wait_ge(c.s_dv, dv_amem)
                    if prev is not None:
                        t.wait_ge(c.s_ac, prev["exp_e"][7])
                    if prev2 is not None:
                        t.wait_ge(c.s_dv, prev2["tmp2_o"])
                else:
                    t.wait_ge(c.s_ac, cur["exp_e"][tp - 1])
                    if tp == 1 and prev is not None:
                        t.wait_ge(c.s_ac, prev["exp_o"][7])
                kvc = cur["oc"] % 4
                qcol = 512 * cur["oc"]
                for i, (pb, ps) in enumerate((
                        (0, c.psA[:, 0:512]),
                        (DH, c.psB[:, 0:512]),
                        (0, c.psA[:, 512:1024]),
                        (DH, c.psB[:, 512:1024]))):
                    tc = 2 * tp + (i // 2)
                    if i == 1:
                        ow = cur["exp_o"][tp - 1] if tp >= 1 else (
                            prev["exp_o"][7] if prev is not None else None)
                        if ow is not None:
                            t.wait_ge(c.s_ac, ow)
                    ins = t.matmul(
                        ps,
                        c.kT[pb:pb + DH, S * kvc + 128 * tc:S * kvc + 128 * (tc + 1)],
                        c.qT[pb:pb + DH, qcol:qcol + 512],
                        start=True, stop=True, skip_group_check=True)
                    ins.then_inc(c.s_pe, 1)
                    bump(c, "s_pe")
                    if i == 2:
                        cur["sc_e"][tp] = c.s_pe_n
                    elif i == 3:
                        cur["sc_o"][tp] = c.s_pe_n
        if prev is not None:
            t.wait_ge(c.s_dv, prev["zcopy_o"])
            t.matmul(c.psB[0:DH, 1024:1536], c.ones_row[0:1, 0:DH], rz_o,
                     start=True, stop=True).then_inc(c.s_pe, 1)
            prev["bcast_o"] = bump(c, "s_pe")
    bigblk.tensor(tb)

    def sb(s):
        sp = EngProxy(c, s, "s_ac")
        for tp in range(8):
            if cur is not None:
                s.wait_ge(c.s_pe, cur["sc_e"][tp])
                if tp == 0 and prev2 is not None:
                    s.wait_ge(c.s_pe, prev2["pv_e_done"])
                    s.wait_ge(c.s_pe, prev2["pv_o_done"])
                sp.activation(cur["pbuf_e"][:, 1024 * tp:1024 * (tp + 1)],
                              c.psA[:, 0:1024], AF.Exp, scale=SCALE)
                s.wait_ge(c.s_pe, cur["sc_o"][tp])
                sp.activation(cur["pbuf_o"][:, 1024 * tp:1024 * (tp + 1)],
                              c.psB[:, 0:1024], AF.Exp, scale=SCALE)
    bigblk.scalar(sb)

    if prev is not None:
        oc = prev["oc"]

        def vb(v):
            vp = EngProxy(c, v, "s_dv")
            v.wait_ge(c.s_pe, prev["pv_e_done"])
            vp.tensor_copy(rz_e, c.psC[64:65, :])
            v.wait_ge(c.s_pe, prev["bcast_e"])
            vp.reciprocal(c.rzB_sb[0:DH, :], c.psA[0:DH, 1024:1536])
            vp.scalar_tensor_tensor(c.tmp2a[0:DH, :], c.psC[0:DH, :],
                                    c.w1m_cT[0:DH, 0:1], c.rzB_sb[0:DH, :],
                                    ALU.mult, ALU.mult)
            vp.tensor_tensor(c.xblk[0:DH, 512 * oc:512 * (oc + 1)],
                             c.xblk[0:DH, 512 * oc:512 * (oc + 1)],
                             c.tmp2a[0:DH, :], ALU.add)
            v.wait_ge(c.s_pe, prev["pv_o_done"])
            vp.tensor_copy(rz_o, c.psD[64:65, :])
            v.wait_ge(c.s_pe, prev["bcast_o"])
            if prev2 is not None and prev2.get("acc16"):
                v.wait_ge(c.s_acc, prev2["acc16"])
            vp.reciprocal(c.rzB_sb[0:DH, :], c.psB[0:DH, 1024:1536])
            vp.scalar_tensor_tensor(c.tmp2b[0:DH, :], c.psD[0:DH, :],
                                    c.w1m_cT[0:DH, 0:1], c.rzB_sb[0:DH, :],
                                    ALU.mult, ALU.mult)
        bigblk.vector(vb)
        assert c.s_dv_n == dv_base + 7

        def gb(g):
            g.wait_ge(c.s_dv, prev["tmp2_o"])
            g.dma_start(c.xblk[DH:P, 512 * oc:512 * (oc + 1)], c.tmp2b[0:DH, :],
                        accum_op=ALU.add).then_inc(c.s_acc, 16)
            if prev2 is not None:
                oc2 = prev2["oc"]
                g.wait_ge(c.s_dv, prev2["x2e"])
                g.wait_ge(c.s_sc, c.s_sc_n)
                g.dma_start(c.x2_scr[:, oc2, c.cur_scol:c.cur_scol + BLK],
                            c.xblk[:, 512 * oc2:512 * (oc2 + 1)]
                            ).then_inc(c.s_sc, 16)
        bigblk.gpsimd(gb)
        if prev2 is not None:
            bump(c, "s_sc", 16)
        prev["acc16"] = bump(c, "s_acc", 16)



def emit_phase_c(c, bigblk, st):
    nc = c.nc
    scol = st * BLK

    bk = bigblk
    if st == 0:
        dv_prev = c.s_dv_n
        gp_prev = c.s_gp_n
        sc_prev = c.s_sc_n
        cv_tot = c.s_cv_n
        load_vals = [c.s_ob_n[b] + 16 for b in range(4)]

        @bk.sync
        def _(sy):
            sy.wait_ge(c.s_dv, dv_prev)
            sy.wait_ge(c.s_gp, gp_prev)
            sy.wait_ge(c.s_sc, sc_prev)
            sy.wait_ge(c.s_cv, cv_tot)
            for b in range(4):
                sy.dma_start(
                    c.acc[:, :].rearrange("p (c s) -> p c s", c=NCH)[:, 4 * b:4 * b + 4, :],
                    c.x2_scr[:, 4 * b:4 * b + 4, scol:scol + BLK]
                ).then_inc(c.s_ob[b], 16)
        for b in range(4):
            c.s_ob_n[b] += 16

        def _prew(eng):
            eng.wait_ge(c.s_cv, cv_tot)

        def _chw(eng, ch):
            eng.wait_ge(c.s_ob[ch // 4], load_vals[ch // 4])

        vb, sb, tb = make_ln_bodies(
            c, c.acc, c.xln2T, (c.ctbf0, c.ctbf1, c.ctsq0, c.ctsq1, c.ctmp),
            pre_wait_v=_prew,
            pre_wait_s=_prew,
            chunk_wait_v=_chw,
            chunk_wait_s=_chw,
        )
        bk.vector(vb)
        bk.scalar(sb)
        bk.tensor(tb)
        ln_done = c.s_dv_n
        apply_gp_end = None
    else:
        store_vals = list(c.s_ob_n)
        load_vals = [v + 16 for v in store_vals]

        @bk.sync
        def _(sy):
            for b in range(4):
                sy.wait_ge(c.s_ob[b], store_vals[b])
                sy.dma_start(
                    c.acc[:, :].rearrange("p (c s) -> p c s", c=NCH)[:, 4 * b:4 * b + 4, :],
                    c.x2_scr[:, 4 * b:4 * b + 4, scol:scol + BLK]
                ).then_inc(c.s_ob[b], 16)
        for b in range(4):
            c.s_ob_n[b] += 16

        c.acc_load_vals = load_vals
        ln_done = None

    bk2 = bigblk
    if True:
        @bk2.tensor
        def _(t):
            if st == 0:
                t.wait_ge(c.s_dv, ln_done)
            else:
                t.wait_ge(c.s_dv, c.apply_dv_end)
            for s4 in range(4):
                for kc in range(NCH):
                    ins = t.matmul(c.psC[:, 8 * s4:8 * (s4 + 1)],
                                   c.xln2T[:, 512 * kc + 128 * s4:512 * kc + 128 * (s4 + 1)],
                                   c.routerT_sb[:, 8 * kc:8 * (kc + 1)],
                                   start=(kc == 0), stop=(kc == NCH - 1),
                                   skip_group_check=True)
            ins.then_inc(c.s_pe, 1)
        bump(c, "s_pe")
        pe_logits = c.s_pe_n

        @bk2.scalar
        def _(s):
            sp = EngProxy(c, s, "s_ac")
            s.wait_ge(c.s_pe, pe_logits)
            sp.activation(c.exp32[:, :], c.psC[:, 0:32], AF.Exp)
        ac_exp = c.s_ac_n
        c.top2_ac_exp = ac_exp
    comb_dv = 0

    if "combine" in c.taps:
        _tap(c, bigblk, f"combine_{st}", c.combine, [P, 32], F32)
    if "xln2T" in c.taps:
        _tap(c, bigblk, f"xln2T_{st}", c.xln2T, [P, NCH * 512], BF16)

    order = [8, 9] + list(range(8))
    for pos, e in enumerate(order):
        _emit_expert(c, bigblk, st, e, comb_dv,
                     last=(pos == NEXP - 1),
                     st_last=(st == NBLK - 1),
                     gate_preloaded=(pos == 0 and st > 0),
                     first=(pos == 0))
        if pos == 0:
            for s4 in range(4):
                _emit_top2(c, bigblk, s4, c.top2_ac_exp)
        if pos == 5 and st < 3:
            emit_c_stats_early(c, bigblk, st + 1)
            emit_ln_rows_early(c, bigblk)
        if pos == 6 and st < 3:
            emit_ln_bcast_early(c, bigblk)


def _emit_top2(c, bk, s4, ac_exp):
    m1 = c.r1a[:, 1:2]
    m2 = c.r1a[:, 2:3]
    r12 = c.r1a[:, 3:4]
    sl = slice(8 * s4, 8 * (s4 + 1))
    X = c.exp32[:, sl]

    def vbody2(v):
        vp = EngProxy(c, v, "s_dv")
        v.wait_ge(c.s_ac, ac_exp)
        vp.reduce_sum(m1, X, AX)
        vp.reciprocal(m2, m1)
        vp.tensor_scalar(X, X, m2, None, ALU.mult)
        vp.tensor_reduce(m1, X, AX, ALU.max)
        vp.tensor_scalar(c.r8a[:, :], X, m1, None, ALU.is_ge)
        vp.scalar_tensor_tensor(c.r8b[:, :], c.r8a[:, :], -1e30, X,
                                ALU.mult, ALU.add)
        vp.tensor_reduce(m2, c.r8b[:, :], AX, ALU.max)
        vp.tensor_scalar(c.r8a[:, :], X, m2, None, ALU.is_ge)
        vp.tensor_tensor(m1, m1, m2, ALU.add)
        vp.reciprocal(r12, m1)
        vp.tensor_tensor(c.r8b[:, :], X, c.r8a[:, :], ALU.mult)
        vp.tensor_scalar(c.combine[:, sl], c.r8b[:, :], r12, None, ALU.mult)
    bk.vector(vbody2)


def _emit_expert(c, bigblk, st, e, comb_dv, last=False, st_last=False,
                 gate_preloaded=False, first=False):
    nc = c.nc
    shared = e >= 8
    half = e - 8
    par = e % 2
    sl_g = (c.slot0, c.slot1)[par]
    sl_u = (c.slot0, c.slot1)[1 - par]
    sl_d = sl_g

    gsrc = c.gu_scr[:, 32 * e:32 * e + 16, :]
    usrc = c.gu_scr[:, 32 * e + 16:32 * e + 32, :]
    dsrc = c.dn_scr[:, 11 * e:11 * e + 11, :]

    bk = bigblk
    if True:
        if gate_preloaded:
            gate_w = c.pre_gate_w
        else:
            gate_slot_free = c.u_pe_end
            cv_done = c.s_cv_n

            @bk.sync
            def _(sy):
                sy.wait_ge(c.s_cv, cv_done)
                sy.wait_ge(c.s_pe, gate_slot_free)
                for j in range(4):
                    sy.dma_start(
                        sl_g[:, :].rearrange("p (ch i) -> p ch i", i=IFF)[:, 4 * j:4 * j + 4, :],
                        gsrc[:, 4 * j:4 * j + 4, :]
                    ).then_inc(c.s_wg, 16)
            bump(c, "s_wg", 64)
            gate_w = c.s_wg_n

        if not shared:
            dv_snap = c.s_dv_n
            ac_cb_snap = c.bcast_ac_done

            @bk.tensor
            def _(t):
                t.wait_ge(c.s_dv, dv_snap)
                t.wait_ge(c.s_ac, ac_cb_snap)
                for s4 in range(4):
                    ins = t.transpose(c.psD[0:1, 128 * s4:128 * (s4 + 1)],
                                      c.combine[:, 8 * s4 + e:8 * s4 + e + 1],
                                      c.ident[:, :])
                ins.then_inc(c.s_pe, 1)
            bump(c, "s_pe")
            pe_tr = c.s_pe_n

            @bk.vector
            def _(v):
                vp = EngProxy(c, v, "s_dv")
                v.wait_ge(c.s_pe, pe_tr)
                vp.tensor_copy(c.cb_row[0:1, :], c.psD[0:1, :])
            dv_cb = c.s_dv_n

            @bk.tensor
            def _(t2):
                t2.wait_ge(c.s_dv, dv_cb)
                t2.matmul(c.psD[:, :], c.ones_row[0:1, :], c.cb_row[0:1, :],
                          start=True, stop=True).then_inc(c.s_pe, 1)
            bump(c, "s_pe")
            pe_cbB = c.s_pe_n

            @bk.vector
            def _(v2):
                vp = EngProxy(c, v2, "s_dv")
                v2.wait_ge(c.s_pe, pe_cbB)
                vp.tensor_copy(c.cbB[:, :], c.psD[:, :])
            c.last_cb_dv = c.s_dv_n
    cb_dv = c.s_dv_n

    bk2 = bigblk
    if True:
        pe0 = c.s_pe_n
        ac0 = c.s_ac_n
        dv_snap = c.s_dv_n
        ac_snap = c.s_ac_n
        gp_snap = c.s_gp_n

        @bk2.tensor
        def _(t):
            pings = (c.psA[:, 0:512], c.psA[:, 512:1024])
            t.wait_ge(c.s_dv, dv_snap)
            t.wait_ge(c.s_ac, ac_snap)
            t.wait_ge(c.s_gp, gp_snap)
            t.wait_ge(c.s_wg, gate_w)
            for ic in range(ICH):
                if ic >= 2:
                    t.wait_ge(c.s_ac, ac0 + ic - 1)
                for kc in range(NCH):
                    ins = t.matmul(pings[ic % 2],
                                   sl_g[:, IFF * kc + 128 * ic:IFF * kc + 128 * (ic + 1)],
                                   c.xln2T[:, 512 * kc:512 * (kc + 1)],
                                   start=(kc == 0), stop=(kc == NCH - 1))
                ins.then_inc(c.s_pe, 1)
                bump(c, "s_pe")

        @bk2.scalar
        def _(s):
            sp = EngProxy(c, s, "s_ac")
            pings = (c.psA[:, 0:512], c.psA[:, 512:1024])
            for ic in range(ICH):
                s.wait_ge(c.s_pe, pe0 + ic + 1)
                if shared:
                    sp.activation(c.silu_g[:, 512 * ic:512 * (ic + 1)], pings[ic % 2],
                                  AF.Silu,
                                  bias=c.gb_col[:, 11 * half + ic:11 * half + ic + 1])
                else:
                    sp.activation(c.silu_g[:, 512 * ic:512 * (ic + 1)], pings[ic % 2],
                                  AF.Silu)

        @bk2.sync
        def _(sy):
            sy.wait_ge(c.s_pe, pe0)
            for j in range(4):
                sy.dma_start(
                    sl_u[:, :].rearrange("p (ch i) -> p ch i", i=IFF)[:, 4 * j:4 * j + 4, :],
                    usrc[:, 4 * j:4 * j + 4, :]
                ).then_inc(c.s_wu, 16)
        bump(c, "s_wu", 64)
    up_w = c.s_wu_n
    g_pe_end = c.s_pe_n
    ac_g_end = c.s_ac_n

    bk3 = bigblk
    if True:
        pe0 = c.s_pe_n
        dv0 = c.s_dv_n

        @bk3.tensor
        def _(t):
            pings = (c.psA[:, 1024:1536], c.psB[:, 0:512])
            t.wait_ge(c.s_wu, up_w)
            t.wait_ge(c.s_dv, dv0)
            for ic in range(ICH):
                if ic >= 2:
                    t.wait_ge(c.s_dv, dv0 + ic - 1)
                for kc in range(NCH):
                    ins = t.matmul(pings[ic % 2],
                                   sl_u[:, IFF * kc + 128 * ic:IFF * kc + 128 * (ic + 1)],
                                   c.xln2T[:, 512 * kc:512 * (kc + 1)],
                                   start=(kc == 0), stop=(kc == NCH - 1))
                ins.then_inc(c.s_pe, 1)
                bump(c, "s_pe")
        c.u_pe_end = c.s_pe_n

        @bk3.sync
        def _(sy):
            sy.wait_ge(c.s_pe, g_pe_end)
            for j in range(4):
                sy.dma_start(
                    sl_d[:, :].rearrange("p (ch o) -> p ch o", o=D)[:, 3 * j:min(3 * j + 3, ICH), :],
                    dsrc[:, 3 * j:min(3 * j + 3, ICH), :]
                ).then_inc(c.s_wd, 16)
        bump(c, "s_wd", 64)
        down_w = c.s_wd_n

        if shared:
            @bk3.vector
            def _(v):
                vp = EngProxy(c, v, "s_dv")
                pings = (c.psA[:, 1024:1536], c.psB[:, 0:512])
                v.wait_ge(c.s_ac, ac_g_end)
                for ic in range(ICH):
                    v.wait_ge(c.s_pe, pe0 + ic + 1)
                    vp.scalar_tensor_tensor(c.h_sb[:, 512 * ic:512 * (ic + 1)],
                                            pings[ic % 2],
                                            c.ub_col[:, 11 * half + ic:11 * half + ic + 1],
                                            c.silu_g[:, 512 * ic:512 * (ic + 1)],
                                            ALU.add, ALU.mult)
        else:
            @bk3.vector
            def _(v):
                vp = EngProxy(c, v, "s_dv")
                pings = (c.psA[:, 1024:1536], c.psB[:, 0:512])
                v.wait_ge(c.s_ac, ac_g_end)
                for ic in range(ICH):
                    v.wait_ge(c.s_pe, pe0 + ic + 1)
                    vp.tensor_tensor(c.h_sb[:, 512 * ic:512 * (ic + 1)],
                                     c.silu_g[:, 512 * ic:512 * (ic + 1)],
                                     pings[ic % 2], ALU.mult)

            @bk3.gpsimd
            def _(g):
                gp = EngProxy(c, g, "s_gp")
                g.wait_ge(c.s_dv, cb_dv)
                for ic in range(ICH):
                    g.wait_ge(c.s_dv, dv0 + ic + 1)
                    gp.tensor_tensor(c.h_sb[:, 512 * ic:512 * (ic + 1)],
                                     c.h_sb[:, 512 * ic:512 * (ic + 1)],
                                     c.cbB[:, :], ALU.mult)
    h_dv = c.s_dv_n
    h_gp = c.s_gp_n

    if last and not st_last:
        cv_done2 = c.s_cv_n
        upe = c.u_pe_end
        ngsrc = c.gu_scr[:, 32 * 8:32 * 8 + 16, :]

        @bigblk.sync
        def _(sy):
            sy.wait_ge(c.s_cv, cv_done2)
            sy.wait_ge(c.s_pe, upe)
            for j in range(4):
                sy.dma_start(
                    c.slot0[:, :].rearrange("p (ch i) -> p ch i", i=IFF)[:, 4 * j:4 * j + 4, :],
                    ngsrc[:, 4 * j:4 * j + 4, :]
                ).then_inc(c.s_wg, 16)
        bump(c, "s_wg", 64)
        c.pre_gate_w = c.s_wg_n

        dv0_pred = c.s_dv_n
        xs_base = [c.s_xs0_n, c.s_xs1_n]
        scol_n = (st + 1) * BLK

        @bigblk.sync
        def _(sy):
            for ch in range(NCH):
                p = ch % 2
                sem = (c.s_xs0, c.s_xs1)[p]
                if ch >= 2:
                    sy.wait_ge(sem, xs_base[p] + 16 * (ch // 2))
                if ch >= 4:
                    sy.wait_ge(c.s_dv, dv0_pred + 3 * (ch - 4) + 3)
                sy.dma_start(c.xs_bufs[ch % 4][:, :],
                             c.x2_scr[:, ch, scol_n:scol_n + BLK]
                             ).then_inc(sem, 16)
        bump(c, "s_xs0", 128)
        bump(c, "s_xs1", 128)
        c.stream_dv0 = dv0_pred
        c.stream_xs_base = xs_base

    bk4 = bigblk
    if True:
        pe0 = c.s_pe_n
        dv0 = c.s_dv_n
        stream = last and not st_last
        nops = 3 if stream else 1
        if stream:
            assert dv0 == c.stream_dv0

        @bk4.tensor
        def _(t):
            pings = (c.psB[:, 512:1024], c.psB[:, 1024:1536])
            t.wait_ge(c.s_wd, down_w)
            t.wait_ge(c.s_dv, h_dv)
            if not shared:
                t.wait_ge(c.s_gp, h_gp)
            for och in range(NCH):
                if och >= 2:
                    t.wait_ge(c.s_dv, dv0 + nops * (och - 2) + 1)
                for ic in range(ICH):
                    ins = t.matmul(pings[och % 2],
                                   sl_d[:, D * ic + 128 * och:D * ic + 128 * (och + 1)],
                                   c.h_sb[:, 512 * ic:512 * (ic + 1)],
                                   start=(ic == 0), stop=(ic == ICH - 1))
                ins.then_inc(c.s_pe, 1)
                bump(c, "s_pe")

        @bk4.vector
        def _(v):
            vp = EngProxy(c, v, "s_dv")
            pings = (c.psB[:, 512:1024], c.psB[:, 1024:1536])
            for och in range(NCH):
                v.wait_ge(c.s_pe, pe0 + och + 1)
                if first and st > 0:
                    v.wait_ge(c.s_ob[och // 4], c.acc_load_vals[och // 4])
                if last:
                    vp.scalar_tensor_tensor(c.acc[:, 512 * och:512 * (och + 1)],
                                            pings[och % 2],
                                            c.db_col[:, och:och + 1],
                                            c.acc[:, 512 * och:512 * (och + 1)],
                                            ALU.add, ALU.add)
                else:
                    vp.tensor_tensor(c.acc[:, 512 * och:512 * (och + 1)], pings[och % 2],
                                     c.acc[:, 512 * och:512 * (och + 1)], ALU.add)
                if stream:
                    if och == 0:
                        v.wait_ge(c.s_ac, c.bcast_ac_done)
                    p = och % 2
                    v.wait_ge((c.s_xs0, c.s_xs1)[p],
                              c.stream_xs_base[p] + 16 * (och // 2 + 1))
                    vp.tensor_tensor(c.ctmp[:, :], c.xs_bufs[och % 4][:, :],
                                     c.aB_sb[:, :], ALU.mult)
                    vp.tensor_tensor(c.xln2T[:, 512 * och:512 * (och + 1)],
                                     c.ctmp[:, :], c.bB_sb[:, :], ALU.add)
        if stream:
            c.apply_dv_end = dv0 + 3 * NCH

        if last:
            scol = st * BLK
            lane_vals = list(c.s_ob_n)

            @bk4.sync
            def _(sy):
                for b in range(4):
                    sy.wait_ge(c.s_dv, dv0 + nops * (4 * b + 3) + 1)
                    sy.wait_ge(c.s_ob[b], lane_vals[b])
                    sy.dma_start(
                        c.outT.rearrange("(c p) s -> p c s", p=P)[:, 4 * b:4 * b + 4, scol:scol + BLK],
                        c.acc[:, :].rearrange("p (c s) -> p c s", c=NCH)[:, 4 * b:4 * b + 4, :]
                    ).then_inc(c.s_ob[b], 16)
            for b in range(4):
                c.s_ob_n[b] += 16


def emit_epilogue(c, bigblk):
    nc = c.nc
    o_total = c.s_o_n
    lane_vals = list(c.s_ob_n)
    bk = bigblk
    if True:
        @bk.sync
        def _(sy):
            sy.wait_ge(c.s_o, o_total)
            for b in range(4):
                sy.wait_ge(c.s_ob[b], lane_vals[b])


def _tap(c, bigblk, name, sb, shape, dtype):
    nc = c.nc
    t = nc.dram_tensor("tap_" + name, list(shape), dtype, kind="ExternalOutput").ap()
    bk = bigblk
    if True:
        dvv = c.s_dv_n
        scv = c.s_sc_n
        accv = c.s_acc_n
        o0 = c.s_o_n

        @bk.sync
        def _(sy):
            sy.wait_ge(c.s_dv, dvv)
            sy.wait_ge(c.s_sc, scv)
            sy.wait_ge(c.s_acc, accv)
            sy.dma_start(t, sb[:, :]).then_inc(c.s_o, 16)
            sy.wait_ge(c.s_o, o0 + 16)
        bump(c, "s_o", 16)
    c.tap_list.append(name)



def _host_inputs(L0_attn_x, L0_attn_Wq, L0_attn_Wk, L0_attn_Wv, L0_attn_memory,
                 L0_attn_memory_norm, L0_attn_memory_weight, L0_ffn_router_w,
                 L0_ffn_gate_w, L0_ffn_up_w, L0_ffn_down_w, L0_ffn_s_gate_w,
                 L0_ffn_s_gate_b, L0_ffn_s_up_w, L0_ffn_s_up_b, L0_ffn_s_down_w,
                 L0_ffn_s_down_b):
    f = np.float32
    shared = dict(
        wqT=np.ascontiguousarray(np.asarray(L0_attn_Wq, f).T),
        wkT=np.ascontiguousarray(np.asarray(L0_attn_Wk, f).T),
        wvT=np.ascontiguousarray(np.asarray(L0_attn_Wv, f).T),
        mem=np.ascontiguousarray(np.asarray(L0_attn_memory, f)),
        memnorm=np.ascontiguousarray(np.asarray(L0_attn_memory_norm, f).reshape(-1)),
        mweight=np.ascontiguousarray(np.asarray(L0_attn_memory_weight, f).reshape(1)),
        routerT=np.ascontiguousarray(np.asarray(L0_ffn_router_w, f).T),
        gate=np.ascontiguousarray(np.asarray(L0_ffn_gate_w, f)),
        up=np.ascontiguousarray(np.asarray(L0_ffn_up_w, f)),
        down=np.ascontiguousarray(np.asarray(L0_ffn_down_w, f)),
        sgate=np.ascontiguousarray(np.asarray(L0_ffn_s_gate_w, f)),
        sup=np.ascontiguousarray(np.asarray(L0_ffn_s_up_w, f)),
        sdown=np.ascontiguousarray(np.asarray(L0_ffn_s_down_w, f)),
        sgb=np.ascontiguousarray(np.asarray(L0_ffn_s_gate_b, f)),
        supb=np.ascontiguousarray(np.asarray(L0_ffn_s_up_b, f)),
        sdb=np.ascontiguousarray(np.asarray(L0_ffn_s_down_b, f)),
    )
    x = np.asarray(L0_attn_x, f)
    in_maps = []
    for b in range(8):
        m = dict(shared)
        m["xT"] = np.ascontiguousarray(x[b].T)
        in_maps.append(m)
    return in_maps


def kernel(**inputs):
    in_maps = _host_inputs(**inputs)
    nc, _ = build_program()
    res = run_bass_kernel_spmd(nc, in_maps, core_ids=list(range(8)))
    out = np.stack([np.asarray(r["outT"], np.float32).T for r in res.results], axis=0)
    return np.ascontiguousarray(out)
